# version 21
# baseline (speedup 1.0000x reference)
import numpy as np
import ml_dtypes
from contextlib import ExitStack

import concourse.bass as bass
import concourse.mybir as mybir
from concourse.bass_utils import run_bass_kernel_spmd

F32 = mybir.dt.float32
BF16 = mybir.dt.bfloat16
AF = mybir.ActivationFunctionType
ALU = mybir.AluOpType
AX = mybir.AxisListType
BF = ml_dtypes.bfloat16

B, S, D = 8, 4096, 1024
SWAPMASK = [i + 1 - 2 * (i % 2) for i in range(32)]
H, DH, ROT = 16, 64, 32
NE, HID = 8, 4096
NPAIR, NKC, NTC, NIT = 8, 8, 32, 8
PASS, NPASS = 1024, 4
HCW, NHC = 512, 8
GN_N = float(64 * S)


def _gammas():
    return 1.0 - 2.0 ** (-5.0 - np.arange(H, dtype=np.float64))


def make_tables():
    t = np.arange(S, dtype=np.float64)
    inv_freq = 1.0 / (10000.0 ** (np.arange(0, ROT, 2, dtype=np.float64) / ROT))
    freqs = np.repeat(t[:, None] * inv_freq[None, :], 2, axis=-1)
    base = (np.arange(0, ROT, 2, dtype=np.float64) + 0.4 * ROT) / (1.4 * ROT)
    power = (t - S // 2) / 512.0
    scale = np.repeat(base[None, :] ** power[:, None], 2, axis=-1)
    cosv, sinv = np.cos(freqs), np.sin(freqs)
    gamma = _gammas()
    sh = (1.0 - gamma[None, :] ** (t[:, None] + 1)) / (1.0 - gamma[None, :])
    cfac = gamma[None, :] ** (t[:, None] % 512) / np.sqrt(sh)
    bfac = gamma[None, :] ** (-(t[:, None] % 128))
    bsign = np.where(np.arange(ROT) % 2 == 0, -1.0, 1.0)
    TQA = np.zeros((NPAIR, 128, S)); TQB = np.zeros((NPAIR, 128, S))
    TKA = np.zeros((NPAIR, 128, S)); TKB = np.zeros((NPAIR, 128, S))
    A = np.ones((S, 64)); Bq = np.zeros((S, 64))
    A[:, :ROT] = cosv * scale
    Bq[:, :ROT] = bsign[None, :] * (sinv * scale)
    Ak = np.ones((S, 64)); Bk = np.zeros((S, 64))
    Ak[:, :ROT] = cosv / scale
    Bk[:, :ROT] = bsign[None, :] * (sinv / scale)
    for hp in range(NPAIR):
        for half in range(2):
            hh = 2 * hp + half
            sl = slice(half * 64, half * 64 + 64)
            TQA[hp, sl, :] = (A * cfac[:, hh:hh + 1]).T
            TQB[hp, sl, :] = (Bq * cfac[:, hh:hh + 1]).T
            TKA[hp, sl, :] = (Ak * bfac[:, hh:hh + 1]).T
            TKB[hp, sl, :] = (Bk * bfac[:, hh:hh + 1]).T
    return [x.astype(BF) for x in (TQA, TQB, TKA, TKB)]


def make_consts():
    c = {}
    c["tqa"], c["tqb"], c["tka"], c["tkb"] = make_tables()
    m = np.zeros((128, 896), np.float32)
    jj = np.arange(128)[:, None]; cc = np.arange(128)[None, :]
    m[:, 384:512] = (cc >= jj).astype(np.float32)
    m[:, 512:] = 1.0
    c["maskbig"] = m
    c["ident16"] = np.eye(128, dtype=np.float32).astype(BF)
    shf = np.zeros((128, 128), np.float32)
    for i in range(0, 128, 2):
        shf[i + 1, i] = 1.0
        shf[i, i + 1] = 1.0
    c["shf"] = shf.astype(BF)
    e2 = np.zeros((128, 2), np.float32)
    e2[:64, 0] = 1.0; e2[64:, 1] = 1.0
    c["e2"] = e2
    p2 = np.zeros((2, 128), np.float32)
    p2[0, :64] = 1.0; p2[1, 64:] = 1.0
    c["p2"] = p2
    sel8 = np.zeros((8, 8 * 128), np.float32)
    for e in range(8):
        sel8[e, e * 128:(e + 1) * 128] = 1.0
    c["sel8"] = sel8.astype(BF)
    gamma = _gammas()
    gc512 = np.zeros((128, 8), np.float32)
    for hp in range(8):
        gc512[:64, hp] = gamma[2 * hp] ** 512.0
        gc512[64:, hp] = gamma[2 * hp + 1] ** 512.0
    c["gamC"] = gc512
    return c


class EM:
    ENGS = ("pe", "dve", "act", "sp", "pool")

    def __init__(self, nc, st):
        self.nc = nc
        self.st = st
        self.q = {e: [] for e in self.ENGS}
        self.sem = {}
        self.cnt = {}
        self.waited = {e: {} for e in self.ENGS}
        self.resw = {}
        self.resr = {}
        for n in ("pe", "dve", "act"):
            self._mksem(n)

    def _mksem(self, name):
        self.sem[name] = self.st.enter_context(self.nc.semaphore(name))
        self.cnt[name] = 0

    def _need(self, reads, writes):
        need = {}
        def add(s, v):
            if v > need.get(s, 0):
                need[s] = v
        for r in reads:
            w = self.resw.get(r)
            if w:
                add(*w)
        for r in writes:
            w = self.resw.get(r)
            if w:
                add(*w)
            for s, v in self.resr.get(r, {}).items():
                add(s, v)
        return need

    def _waits(self, eng, need):
        for s in sorted(need):
            v = need[s]
            if eng == "pe" and s == "pe":
                continue
            if v <= self.waited[eng].get(s, 0):
                continue
            self.waited[eng][s] = v
            sem = self.sem[s]
            self.q[eng].append(lambda h, sem=sem, v=v: h.wait_ge(sem, v))

    def op(self, eng, fn, r=(), w=()):
        self._waits(eng, self._need(r, w))
        self.cnt[eng] += 1
        v = self.cnt[eng]
        sem = self.sem[eng]
        self.q[eng].append(lambda h, fn=fn, sem=sem: fn(h).then_inc(sem, 1))
        for x in w:
            self.resw[x] = (eng, v)
            self.resr[x] = {}
        for x in r:
            self.resr.setdefault(x, {})[eng] = v

    def dma(self, queue, semname, fn, r=(), w=()):
        self.dma_group(queue, semname, [(fn, r, w)])

    def dma_group(self, queue, semname, items):
        if semname not in self.sem:
            self._mksem(semname)
        allw = []
        for fn, r, w in items:
            self._waits(queue, self._need(r, w))
            self.cnt[semname] += 16
            sem = self.sem[semname]
            self.q[queue].append(lambda h, fn=fn, sem=sem: fn(h).then_inc(sem, 16))
            allw.extend(w)
            for x in r:
                self.resr.setdefault(x, {})[semname] = self.cnt[semname]
        v = self.cnt[semname]
        for x in allw:
            self.resw[x] = (semname, v)
            self.resr[x] = {}

    def mark(self, res, semname):
        self.resw[res] = (semname, self.cnt.get(semname, 0))
        self.resr[res] = {}

    def flush(self):
        if not any(self.q.values()):
            return
        with self.nc.Block() as blk:
            m = {"pe": blk.tensor, "dve": blk.vector, "act": blk.scalar,
                 "sp": blk.sync, "pool": blk.gpsimd}
            for e in self.ENGS:
                lst = self.q[e]
                if lst:
                    m[e](lambda h, lst=tuple(lst): [f(h) for f in lst])
        self.q = {e: [] for e in self.ENGS}

    def barrier(self):
        self.flush()
        for e in self.ENGS:
            for s in ("pe", "dve", "act"):
                self.waited[e][s] = self.cnt[s]


def build(phase_limit=99, debug=False):
    nc = bass.Bass("TRN2", target_bir_lowering=False)
    ap = {}
    def din(name, shape, dt=F32):
        ap[name] = nc.dram_tensor(name, list(shape), dt, kind="ExternalInput").ap()
    din("x", (S, D))
    for n in ("wq", "wk", "wv", "wg", "wo"):
        din(n, (D, D))
    din("bvrow", (1, D), BF16)
    din("borow", (1, D), BF16)
    din("gbrow", (1, 8), BF16)
    din("bT", (128, 24))
    din("b1T", (128, 32 * 8))
    din("b2sb", (8, D), BF16)
    din("gnw2", (2, 8)); din("gnb2", (2, 8))
    din("gate_wT", (D, 8))
    din("w1", (NE, D, HID)); din("w2", (NE, HID, D))
    for n in ("tqa", "tqb", "tka", "tkb"):
        din(n, (NPAIR, 128, S), BF16)
    din("maskbig", (128, 896))
    din("ident16", (128, 128), BF16)
    din("shf", (128, 128), BF16)
    din("e2", (128, 2)); din("p2", (2, 128))
    din("sel8", (8, 8 * 128), BF16)
    din("gamC", (128, 8))
    out = nc.dram_tensor("out", [S, D], F32, kind="ExternalOutput").ap()
    gcs = nc.dram_tensor("gcs", [128, NPAIR * S], BF16, kind="Internal").ap()
    xns = nc.dram_tensor("xns", [S, D], BF16, kind="Internal").ap()
    dbg = {}
    if debug:
        for n, shape, dt in (("d_xnT", [128, NKC * S], BF16), ("d_gtm", [128, 256], BF16),
                             ("d_gt8", [8, S], BF16), ("d_gcs", [128, NPAIR * S], BF16)):
            dbg[n] = nc.dram_tensor(n, shape, dt, kind="ExternalOutput").ap()

    gam = _gammas()

    st = ExitStack()
    with st:
        em = EM(nc, st)
        sb = lambda name, shape, dt: st.enter_context(nc.sbuf_tensor(name, list(shape), dt))

        xnT = sb("xnT", [128, NKC * S], BF16)
        ident16 = sb("c_id16", [128, 128], BF16)
        e2 = sb("c_e2", [128, 2], F32)
        p2 = sb("c_p2", [2, 128], F32)
        sel8 = sb("c_sel8", [8, 8 * 128], BF16)
        maskbig = sb("c_mask", [128, 896], F32)
        bT = sb("c_bT", [128, 24], F32)
        b1T = sb("c_b1T", [128, 256], F32)
        b2sb = sb("c_b2", [8, D], BF16)
        bvrow = sb("c_bv", [1, D], BF16)
        borow = sb("c_bo", [1, D], BF16)
        gbrow = sb("c_gb", [1, 8], BF16)
        gnw2 = sb("c_gnw", [2, 8], F32)
        gnb2 = sb("c_gnb", [2, 8], F32)
        gwT = sb("c_gwT", [128, NKC * 8], BF16)
        ones16 = sb("c_ones", [1, 512], BF16)
        epsT = sb("c_eps", [128, 1], F32)
        g_tm = sb("g_tm", [128, NTC * 8], BF16)
        gT8 = sb("gT8", [8, S], BF16)
        gamC = sb("c_gamC", [128, 8], F32)

        citems = []
        for dst, src in ((maskbig, "maskbig"), (e2, "e2"), (p2, "p2"), (bT, "bT"),
                         (b1T, "b1T"), (gnw2, "gnw2"), (gnb2, "gnb2"),
                         (ident16, "ident16"), (sel8, "sel8"),
                         (b2sb, "b2sb"), (bvrow, "bvrow"), (borow, "borow"),
                         (gbrow, "gbrow"), (gamC, "gamC")):
            citems.append((lambda h, d=dst, s=ap[src]: h.dma_start(d[:, :], s[:, :]), (), ()))
        em.dma_group("sp", "dcs", citems)
        em.mark("consts", "dcs")
        em.dma("pool", "dcsg", lambda h: h.dma_start(
            gwT[:, :].rearrange("p (c e) -> p c e", c=8),
            ap["gate_wT"].rearrange("(c p) e -> p c e", p=128),
            allow_slow_non_contiguous=True), w=("gwT",))
        em.op("dve", lambda h: h.memset(ones16[:, :], 1.0), w=("ones16",))
        em.op("dve", lambda h: h.memset(epsT[:, :], 1e-5), w=("epsT",))

        with ExitStack() as ph:
            psb = lambda name, shape, dt: ph.enter_context(nc.sbuf_tensor(name, list(shape), dt))
            pps = lambda name, shape, dt: ph.enter_context(nc.psum_tensor(name, list(shape), dt))
            xld = [psb(f"xld{i}", [128, D], F32) for i in range(4)]
            xn_tm = [psb(f"xntm{i}", [128, D], BF16) for i in range(2)]
            lnstb = [psb(f"lnst{i}", [128, 8], F32) for i in range(3)]
            lnbb = [psb(f"lnb{i}", [128, 12], F32) for i in range(3)]

            def ln_stage1(tc):
                pi = tc % 4
                t3 = tc % 3
                lnst = lnstb[t3]
                LS = f"_{t3}"
                xb = f"xld{pi}"
                em.dma("sp", f"dx{pi}",
                       lambda h, t=tc, b=xld[pi]: h.dma_start(b[:, :], ap["x"][t * 128:(t + 1) * 128, :]),
                       w=(xb,))
                lnb = lnbb[t3]
                em.op("dve", lambda h, b=xld[pi], o=lnb: h.bn_stats(o[:, 0:6], b[:, 0:512]),
                      r=(xb,), w=("lnb" + LS,))
                em.op("dve", lambda h, b=xld[pi], o=lnb: h.bn_stats(o[:, 6:12], b[:, 512:1024]),
                      r=(xb, "lnb" + LS), w=("lnb" + LS,))
                em.op("dve", lambda h, i=lnb: h.bn_aggr(lnst[:, 2:4], i[:, 0:12]),
                      r=("lnb" + LS,), w=("lnst2" + LS, "lnst3" + LS))
                em.op("act", lambda h: h.activation(lnst[:, 4:5], lnst[:, 3:4], AF.Ln, bias=epsT[:, 0:1]),
                      r=("lnst3" + LS, "epsT"), w=("lnst4" + LS,))
                em.op("act", lambda h: h.activation(lnst[:, 5:6], lnst[:, 4:5], AF.Exp, scale=-0.5),
                      r=("lnst4" + LS,), w=("lnst5" + LS,))
            def ln_stage2(tc):
                pi = tc % 4
                xi = tc & 1
                t3 = tc % 3
                lnst = lnstb[t3]
                LS = f"_{t3}"
                xb = f"xld{pi}"
                em.op("dve", lambda h, b=xld[pi], o=xn_tm[xi]: h.tensor_scalar(
                    o[:, :], b[:, :], lnst[:, 2:3], lnst[:, 5:6], ALU.subtract, ALU.mult),
                    r=(xb, "lnst2" + LS, "lnst5" + LS), w=(f"xntm{xi}",))

                em.dma("act", f"dxn{xi}", lambda h, t=tc, b=xn_tm[xi]: h.dma_start(
                    xns[t * 128:(t + 1) * 128, :], b[:, :]), r=(f"xntm{xi}",), w=(f"xnsw{tc}",))
            for tc in range(NTC):
                ln_stage1(tc)
                if tc > 1:
                    ln_stage2(tc - 2)
            ln_stage2(NTC - 2)
            ln_stage2(NTC - 1)
            for kc in range(NKC):
                em.dma("sp", "dxt", lambda h, kc=kc: h.dma_start(
                    xnT[:, kc * S:(kc + 1) * S], xns[:, kc * 128:(kc + 1) * 128], transpose=True),
                    r=tuple(f"xnsw{t}" for t in range(NTC)), w=("xnT",))
        em.barrier()
        if debug:
            em.dma("sp", "ddbg", lambda h: h.dma_start(dbg["d_xnT"][:, :], xnT[:, :]), r=("xnT",))
        if phase_limit < 2:
            em.flush(); _final_waits(em); return nc

        em.barrier()
        if phase_limit < 3:
            em.flush(); _final_waits(em); return nc

        with ExitStack() as ph:
            psb = lambda name, shape, dt: ph.enter_context(nc.sbuf_tensor(name, list(shape), dt))
            pps = lambda name, shape, dt: ph.enter_context(nc.psum_tensor(name, list(shape), dt))
            qT = [psb(f"qT{i}", [128, S], BF16) for i in range(2)]
            kT = [psb(f"kT{i}", [128, S], BF16) for i in range(2)]
            vtm = [psb(f"v{i}", [128, S], BF16) for i in range(2)]
            q_raw = {(wq, i): psb(f"qraw{wq}{i}", [128, 512], BF16)
                     for wq in ("q", "k") for i in range(2)}
            t1b = [psb(f"t1_{i}", [128, 512], BF16) for i in range(2)]
            t2b = [psb(f"t2_{i}", [128, 512], BF16) for i in range(2)]
            tabbuf = [psb(f"tab{i}", [128, 2048], BF16) for i in range(4)]
            s_tile = [psb(f"stile{i}", [128, 512], BF16) for i in range(4)]
            g_full = psb("g_full", [128, S], BF16)
            gcc = [psb(f"gcc{i}", [128, S], BF16) for i in range(2)]
            wsl = {n: [psb(f"w{n}{i}", [128, 1024], BF16) for i in range(2)]
                   for n in ("q", "k", "v", "g")}
            gnst = psb("gnst", [128, 12], F32)
            gnb = psb("gnb", [128, 48], F32)
            gnt = psb("gnt", [2, 8], F32)
            ab_bc = psb("abbc", [128, 2], F32)
            pA = pps("pA", [128, 512], F32)
            pS = pps("pS", [128, 512], F32)
            qs_sb = [psb(f"qs{i}", [128, 512], BF16) for i in range(2)]
            pC = [pps(f"pC{i}", [128, 512], F32) for i in range(2)]
            pR = [pps(f"pR{i}", [128, 512], F32) for i in range(2)]
            pST = pps("pST", [128, 512], F32)
            pTb = pps("pTb", [128, 1024], BF16)
            S32 = psb("S32", [128, 64], F32)
            S16 = psb("S16", [128, 64], BF16)
            ktm = [psb(f"ktm{i}", [128, 512], BF16) for i in range(2)]
            pB = pA

            WMAP = {"q": "wq", "k": "wk", "v": "wv", "g": "wg"}

            def load_pair_w(p):
                pi = p & 1
                items = []
                for n in ("q", "k", "v", "g"):
                    src = ap[WMAP[n]][:, p * 128:(p + 1) * 128].rearrange(
                        "(c p2) m -> p2 c m", p2=128)
                    items.append((lambda h, d=wsl[n][pi], s=src: h.dma_start(
                        d[:, :].rearrange("p (c m) -> p c m", c=8), s),
                                  (), (f"w{n}{pi}",)))
                em.dma_group("pool", f"dw{pi}", items)

            def emit_proj(p):
                pi = p & 1
                load_pair_w(p)
                for it in range(NIT):
                    ti = (p * NIT + it) % 4
                    em.dma_group("sp", f"dt{ti}", [
                        (lambda h, d=tabbuf[ti], s=ap[tn][p][:, it * 512:(it + 1) * 512], j=j:
                         h.dma_start(d[:, j * 512:(j + 1) * 512], s), (), (f"tab{ti}",))
                        for j, tn in enumerate(("tqa", "tqb", "tka", "tkb"))])
                    for which, tcol in (("q", 0), ("k", 2)):
                        dstT = (qT if which == "q" else kT)[pi]
                        wbuf = wsl[which][pi]
                        wres = f"w{which}{pi}"
                        bcol = p * 3 + (0 if which == "q" else 1)
                        pP = pA if (it & 1) == 0 else pS
                        pPr = "pA" if (it & 1) == 0 else "pS"
                        qi = it & 1
                        qr = q_raw[(which, qi)]
                        qres = f"qraw{which}{qi}"
                        def proj(h, w=wbuf, it=it, pP=pP):
                            last = None
                            for kc in range(NKC):
                                last = h.matmul(pP[:, :], w[:, kc * 128:(kc + 1) * 128],
                                                xnT[:, kc * S + it * 512: kc * S + (it + 1) * 512],
                                                start=(kc == 0), stop=(kc == NKC - 1))
                            return last
                        em.op("pe", proj, r=(wres, "xnT"), w=(pPr,))
                        em.op("act", lambda h, pP=pP, o=qr, b=bcol: h.activation(
                            o[:, :], pP[:, :], AF.Identity, bias=bT[:, b:b + 1]),
                            r=(pPr, "consts"), w=(qres,))
                        em.op("dve", lambda h, i=qr, o=qs_sb[qi]: h.stream_shuffle(
                            o[:, :], i[:, :], SWAPMASK),
                            r=(qres,), w=(f"qs{qi}",))
                        em.op("dve", lambda h, i=qr, tb=tabbuf[ti], tcol=tcol, t1=t1b[qi]: h.tensor_tensor(
                            t1[:, :], i[:, :], tb[:, tcol * 512:(tcol + 1) * 512], ALU.mult),
                            r=(qres, f"tab{ti}"), w=(f"t1_{qi}",))
                        em.op("dve", lambda h, o=qs_sb[qi], tb=tabbuf[ti], tcol=tcol, t2=t2b[qi]: h.tensor_tensor(
                            t2[:, :], o[:, :], tb[:, (tcol + 1) * 512:(tcol + 2) * 512], ALU.mult),
                            r=(f"qs{qi}", f"tab{ti}"), w=(f"t2_{qi}",))
                        em.op("dve", lambda h, o=dstT, it=it, t1=t1b[qi], t2=t2b[qi]: h.tensor_tensor(
                            o[:, it * 512:(it + 1) * 512], t1[:, :], t2[:, :], ALU.add),
                            r=(f"t1_{qi}", f"t2_{qi}"), w=(f"{which}T{pi}",))
                for tc in range(NTC):
                    pP = pA
                    pPr = "pA"
                    sl = (tc % 4) * 128
                    def vproj(h, tc=tc, pP=pP, sl=sl, w=wsl["v"][pi], p=p):
                        for kc in range(NKC):
                            h.matmul(pP[:, sl:sl + 128],
                                     xnT[:, kc * S + tc * 128: kc * S + tc * 128 + 128],
                                     w[:, kc * 128:(kc + 1) * 128], start=(kc == 0), stop=False)
                        return h.matmul(pP[:, sl:sl + 128], ones16[0:1, 0:128],
                                        bvrow[0:1, p * 128:(p + 1) * 128], start=False, stop=True)
                    em.op("pe", vproj, r=(f"wv{pi}", "xnT", "consts", "ones16"),
                          w=((pPr,) if tc % 4 == 0 else (pPr + "x",)))
                    if tc % 4 == 3:
                        em.op("act", lambda h, tc=tc, pP=pP, o=vtm[pi]: h.copy(
                            o[:, (tc - 3) * 128:(tc + 1) * 128], pP[:, :]),
                            r=(pPr, pPr + "x"), w=(f"v{pi}",))

            def emit_retention(p):
                pi = p & 1
                em.op("dve", lambda h: h.memset(S32[:, :], 0.0), w=("S32",))
                em.op("dve", lambda h: h.memset(S16[:, :], 0.0), w=("S16",))
                steps = [(c, half, jc) for c in range(NIT) for half in range(2)
                         for jc in range(4)]
                LOOK = 3

                def emit_mm1_evac(s):
                    c, half, jc, = steps[s]
                    ci = s % 2
                    g = gam[2 * p + half]
                    def mm1(h, half=half, c=c, jc=jc, pCt=pC[ci]):
                        return h.matmul(pCt[:, :],
                                        kT[pi][half * 64:half * 64 + 64,
                                               c * 512 + jc * 128: c * 512 + (jc + 1) * 128],
                                        qT[pi][half * 64:half * 64 + 64, c * 512:(c + 1) * 512],
                                        start=True, stop=True)
                    em.op("pe", mm1, r=(f"kT{pi}", f"qT{pi}"), w=(f"pC{ci}",))
                    scal = float(np.float32(g ** float(-128 * jc)))
                    em.op("dve", lambda h, pCt=pC[ci], o=s_tile[jc], jc=jc, scal=scal:
                          h.scalar_tensor_tensor(o[:, jc * 128:512], pCt[:, jc * 128:512], scal,
                                                 maskbig[:, 384:896 - jc * 128],
                                                 ALU.mult, ALU.mult),
                          r=(f"pC{ci}", "consts"), w=(f"stile{jc}",))

                def emit_mm2(s):
                    c, half, jc = steps[s]
                    ri = c & 1
                    rs = half * 64
                    if jc == 0:
                        if c > 0:
                            def cross(h, half=half, c=c, pRt=pR[ri], rs=rs):
                                return h.matmul(pRt[rs:rs + 64, :],
                                                S16[rs:rs + 64, :],
                                                qT[pi][rs:rs + 64, c * 512:(c + 1) * 512],
                                                start=True, stop=False)
                            em.op("pe", cross, r=("S16", f"qT{pi}"), w=(f"pR{ri}",))
                    def mm2(h, half=half, c=c, jc=jc, st=s_tile[jc], pRt=pR[ri], rs=rs,
                            first=(jc == 0 and c == 0), last=(jc == 3)):
                        return h.matmul(pRt[rs:rs + 64, :],
                                        vtm[pi][:, (c * 4 + jc) * 128 + half * 64:
                                                (c * 4 + jc) * 128 + half * 64 + 64],
                                        st[:, :], start=first, stop=last)
                    em.op("pe", mm2, r=(f"v{pi}", f"stile{jc}", "S16" if jc == 0 else f"v{pi}"),
                          w=(f"pR{ri}",))
                    if jc == 3 and half == 1:
                        em.op("act", lambda h, pRt=pR[ri], o=gcc[pi], c=c: h.copy(
                            o[:, c * 512:(c + 1) * 512], pRt[:, :]),
                            r=(f"pR{ri}",), w=(f"gcc{pi}",))

                def emit_state_prep(c):
                    ki = c & 1
                    kb = ktm[ki]
                    for sub in range(4):
                        em.op("pe", lambda h, c=c, sub=sub: h.transpose(
                            pTb[:, sub * 128:(sub + 1) * 128],
                            kT[pi][:, c * 512 + sub * 128: c * 512 + (sub + 1) * 128],
                            ident16[:, :]),
                            r=(f"kT{pi}", "consts"), w=("pTb",) if sub == 0 else ("pTbx",))
                    for sub in range(4):
                        for half in range(2):
                            g = gam[2 * p + half]
                            scal = float(np.float32(g ** float(512 - 128 * sub)))
                            em.op("act", lambda h, sub=sub, half=half, scal=scal, kb=kb: h.activation(
                                kb[:, sub * 128 + half * 64: sub * 128 + half * 64 + 64],
                                pTb[:, sub * 128 + half * 64: sub * 128 + half * 64 + 64],
                                AF.Copy, scale=scal),
                                r=("pTb", "pTbx"), w=(f"ktm{ki}",))

                def emit_state_upd(c):
                    ki = c & 1
                    kb = ktm[ki]
                    for half in range(2):
                        rs = half * 64
                        def smm(h, c=c, half=half, rs=rs, kb=kb):
                            last = None
                            for sub in range(4):
                                last = h.matmul(pST[rs:rs + 64, 0:64],
                                               kb[:, sub * 128 + rs: sub * 128 + rs + 64],
                                               vtm[pi][:, (c * 4 + sub) * 128 + rs:
                                                       (c * 4 + sub) * 128 + rs + 64],
                                               start=(sub == 0), stop=(sub == 3))
                            return last
                        em.op("pe", smm, r=(f"ktm{ki}", f"v{pi}", "pSTg"), w=(f"pSTh{half}",))
                    em.op("dve", lambda h, p=p: h.scalar_tensor_tensor(
                        S32[:, :], S32[:, :], gamC[:, p:p + 1], pST[:, 0:64],
                        ALU.mult, ALU.add),
                        r=("S32", "pSTh0", "pSTh1", "consts"), w=("S32",))
                    em.op("act", lambda h: h.copy(S16[:, :], S32[:, :]),
                          r=("S32",), w=("S16",))

                emit_state_prep(0)
                emit_state_prep(1)
                n = len(steps)
                for s in range(n + LOOK):
                    if s < n:
                        emit_mm1_evac(s)
                    if s >= LOOK:
                        emit_mm2(s - LOOK)
                        c2, half2, jc2 = steps[s - LOOK]
                        if half2 == 1 and jc2 == 0:
                            emit_state_upd(c2)
                            if c2 + 2 < NIT:
                                emit_state_prep(c2 + 2)

            def emit_gn_gate(p):
                pi = p & 1
                gres = f"gcc{pi}"
                gch = gcc[pi]
                for c8 in range(8):
                    em.op("dve", lambda h, gch=gch, c8=c8: h.bn_stats(
                        gnb[:, c8 * 6:(c8 + 1) * 6], gch[:, c8 * 512:(c8 + 1) * 512]),
                        r=(gres,) + (("gnb",) if c8 else ()), w=("gnb",))
                em.op("dve", lambda h: h.bn_aggr(gnst[:, 2:4], gnb[:, 0:48]),
                      r=("gnb",), w=("gnstmv",))
                em.op("dve", lambda h: h.tensor_scalar(gnst[:, 0:1], gnst[:, 2:3], 4096.0, None, ALU.mult),
                      r=("gnstmv",), w=("gnst0",))
                em.op("dve", lambda h: h.tensor_tensor(gnst[:, 4:5], gnst[:, 2:3], gnst[:, 2:3], ALU.mult),
                      r=("gnstmv",), w=("gnst4",))
                em.op("dve", lambda h: h.tensor_tensor(gnst[:, 4:5], gnst[:, 3:4], gnst[:, 4:5], ALU.add),
                      r=("gnstmv", "gnst4"), w=("gnst4",))
                em.op("dve", lambda h: h.tensor_scalar(gnst[:, 1:2], gnst[:, 4:5], 4096.0, None, ALU.mult),
                      r=("gnst4",), w=("gnst1",))
                em.op("pe", lambda h: h.matmul(pST[0:2, 64:66], e2[:, :], gnst[:, 0:2],
                                               start=True, stop=True),
                      r=("gnst0", "gnst1", "consts", "pSTh0", "pSTh1"), w=("pSTg",))
                for it in range(NIT):
                    pP = pA
                    pPr = "pA"
                    def gproj(h, it=it, pP=pP, w=wsl["g"][pi]):
                        last = None
                        for kc in range(NKC):
                            last = h.matmul(pP[:, :], w[:, kc * 128:(kc + 1) * 128],
                                            xnT[:, kc * S + it * 512: kc * S + (it + 1) * 512],
                                            start=(kc == 0), stop=(kc == NKC - 1))
                        return last
                    em.op("pe", gproj, r=(f"wg{pi}", "xnT"), w=(pPr,))
                    em.op("act", lambda h, pP=pP, o=g_full, it=it, b=p * 3 + 2: h.activation(
                        o[:, it * 512:(it + 1) * 512], pP[:, :], AF.Silu, bias=bT[:, b:b + 1]),
                        r=(pPr, "consts"), w=("g_full",))
                em.op("dve", lambda h: h.tensor_copy(gnt[:, 0:2], pST[0:2, 64:66]),
                      r=("pSTg",), w=("gnt01",))
                em.op("dve", lambda h: h.tensor_scalar(gnt[:, 2:3], gnt[:, 0:1], 1.0 / GN_N, None, ALU.mult),
                      r=("gnt01",), w=("gnt2",))
                em.op("dve", lambda h: h.tensor_tensor(gnt[:, 3:4], gnt[:, 2:3], gnt[:, 2:3], ALU.mult),
                      r=("gnt2",), w=("gnt3",))
                em.op("dve", lambda h: h.scalar_tensor_tensor(gnt[:, 3:4], gnt[:, 1:2], 1.0 / GN_N,
                                                              gnt[:, 3:4], ALU.mult, ALU.subtract),
                      r=("gnt01", "gnt3"), w=("gnt3",))
                em.op("act", lambda h: h.activation(gnt[:, 4:5], gnt[:, 3:4], AF.Ln, bias=epsT[0:2, 0:1]),
                      r=("gnt3", "epsT"), w=("gnt4",))
                em.op("act", lambda h: h.activation(gnt[:, 5:6], gnt[:, 4:5], AF.Exp, scale=-0.5),
                      r=("gnt4",), w=("gnt5",))
                em.op("dve", lambda h, p=p: h.tensor_tensor(gnt[:, 6:7], gnt[:, 5:6],
                                                            gnw2[:, p:p + 1], ALU.mult),
                      r=("gnt5", "consts"), w=("gnt6",))
                em.op("dve", lambda h: h.tensor_tensor(gnt[:, 3:4], gnt[:, 2:3], gnt[:, 6:7], ALU.mult),
                      r=("gnt2", "gnt6"), w=("gnt3",))
                em.op("dve", lambda h, p=p: h.tensor_tensor(gnt[:, 7:8], gnb2[:, p:p + 1],
                                                            gnt[:, 3:4], ALU.subtract),
                      r=("gnt3", "consts"), w=("gnt7",))
                em.op("pe", lambda h: h.matmul(pST[:, 66:68], p2[:, :], gnt[:, 6:8],
                                               start=True, stop=True),
                      r=("gnt6", "gnt7", "consts", "pSTg"), w=("pSTg",))
                em.op("dve", lambda h: h.tensor_copy(ab_bc[:, :], pST[:, 66:68]),
                      r=("pSTg",), w=("ab_bc",))
                em.op("act", lambda h, gch=gch: h.activation(gch[:, :], gch[:, :], AF.Identity,
                                                             scale=ab_bc[:, 0:1], bias=ab_bc[:, 1:2]),
                      r=(gres, "ab_bc"), w=(gres,))
                for it in range(NIT):
                    em.op("dve", lambda h, it=it, gch=gch: h.tensor_tensor(
                        gch[:, it * 512:(it + 1) * 512], gch[:, it * 512:(it + 1) * 512],
                        g_full[:, it * 512:(it + 1) * 512], ALU.mult),
                        r=(gres, "g_full"), w=(gres,))
                em.dma("pool", f"dgc{pi}", lambda h, p=p, gch=gch: h.dma_start(
                    bass.AP(gcs.tensor, p * 128, [[NPAIR * S, 128], [1024, 32], [1, 128]]),
                    gch[:, :].rearrange("q (c m) -> q c m", m=128)), r=(gres,))
                if debug:
                    em.dma("sp", "ddbg", lambda h, p=p, gch=gch: h.dma_start(
                        dbg["d_gcs"][:, p * S:(p + 1) * S], gch[:, :]), r=(gres,))

            for jc in range(1, 4):
                em.op("dve", lambda h, jc=jc: h.memset(s_tile[jc][:, 0:jc * 128], 0.0),
                      w=(f"stile{jc}",))
            emit_proj(0)
            for p in range(1, NPAIR):
                emit_retention(p - 1)
                emit_proj(p)
                emit_gn_gate(p - 1)
            emit_retention(NPAIR - 1)
            emit_gn_gate(NPAIR - 1)
            em.mark("gcs_allA", "dgc0")
            em.mark("gcs_allB", "dgc1")
        em.barrier()
        if phase_limit < 4:
            em.flush(); _final_waits(em); return nc

        with ExitStack() as ph:
            psb = lambda name, shape, dt: ph.enter_context(nc.sbuf_tensor(name, list(shape), dt))
            pps = lambda name, shape, dt: ph.enter_context(nc.psum_tensor(name, list(shape), dt))
            wo_sb = psb("wo_sb", [128, NKC * 1024], BF16)
            gcd = [psb(f"gcd{i}", [128, 1024], BF16) for i in range(4)]
            xl2 = [psb(f"xl2{i}", [128, D], F32) for i in range(4)]
            ostg = [psb(f"ostg{i}", [128, D], F32) for i in range(2)]
            pW = [pps(f"pW{i}", [128, 1024], F32) for i in range(2)]
            pgb = [pps(f"pg{i}", [128, 512], F32) for i in range(2)]
            ptg = pps("ptg", [128, 1024], BF16)
            exb = [psb(f"ex{i}", [128, 8], F32) for i in range(3)]
            mskb = [psb(f"msk{i}", [128, 8], F32) for i in range(3)]
            gstb = [psb(f"gst{i}", [128, 8], F32) for i in range(3)]
            def emit_gate_tc(tc):
                g3 = tc % 3
                ex = exb[g3]
                msk = mskb[g3]
                gst = gstb[g3]
                pg = pgb[tc % 2]
                XS = f"_{g3}"
                PGR = f"pg{tc % 2}"
                def logits(h, t=tc):
                    for kc in range(NKC):
                        h.matmul(pg[:, 0:8], xnT[:, kc * S + t * 128: kc * S + t * 128 + 128],
                                 gwT[:, kc * 8:(kc + 1) * 8], start=(kc == 0), stop=False)
                    return h.matmul(pg[:, 0:8], ones16[0:1, 0:128], gbrow[0:1, :],
                                    start=False, stop=True)
                em.op("pe", logits, r=("xnT", "gwT", "consts", "ones16"), w=(PGR,))
                em.op("dve", lambda h: h.reduce_max(gst[:, 1:2], pg[:, 0:8], axis=AX.X, negate=True),
                      r=(PGR,), w=("gst1" + XS,))
                em.op("act", lambda h: h.activation(ex[:, :], pg[:, 0:8], AF.Exp, bias=gst[:, 1:2],
                                                    accum_out=gst[:, 2:3]),
                      r=(PGR, "gst1" + XS), w=("ex" + XS, "gst2" + XS)),
                em.op("dve", lambda h: h.reciprocal(gst[:, 3:4], gst[:, 2:3]),
                      r=("gst2" + XS,), w=("gst3" + XS,))
                em.op("dve", lambda h: h.reduce_max(gst[:, 4:5], ex[:, :], axis=AX.X),
                      r=("ex" + XS,), w=("gst4" + XS,))
                em.op("dve", lambda h: h.tensor_scalar(msk[:, :], ex[:, :], gst[:, 4:5], None, ALU.is_equal),
                      r=("ex" + XS, "gst4" + XS), w=("msk" + XS,))
                em.op("dve", lambda h: h.scalar_tensor_tensor(msk[:, :], msk[:, :], -1e30, ex[:, :],
                                                              ALU.mult, ALU.add),
                      r=("msk" + XS, "ex" + XS), w=("msk" + XS,))
                em.op("dve", lambda h: h.reduce_max(gst[:, 5:6], msk[:, :], axis=AX.X),
                      r=("msk" + XS,), w=("gst5" + XS,))
                em.op("dve", lambda h: h.tensor_scalar(msk[:, :], ex[:, :], gst[:, 5:6], None, ALU.is_ge),
                      r=("ex" + XS, "gst5" + XS), w=("msk" + XS,))
                em.op("dve", lambda h: h.tensor_tensor(msk[:, :], ex[:, :], msk[:, :], ALU.mult),
                      r=("ex" + XS, "msk" + XS), w=("msk" + XS,))
                em.op("dve", lambda h, t=tc: h.tensor_scalar(g_tm[:, t * 8:(t + 1) * 8], msk[:, :],
                                                             gst[:, 3:4], None, ALU.mult),
                      r=("msk" + XS, "gst3" + XS), w=("g_tm",))
                pass



            em.dma("pool", "dwo", lambda h: h.dma_start(
                wo_sb[:, :].rearrange("p (c n) -> p c n", c=8),
                ap["wo"].rearrange("(c p) n -> p c n", p=128)), w=("wo_sb",))
            for tc in range(NTC):
                pi = tc & 1
                qi = tc % 4
                em.dma("sp", f"dx2{qi}", lambda h, t=tc, b=xl2[qi]: h.dma_start(
                    b[:, :], ap["x"][t * 128:(t + 1) * 128, :]), w=(f"xl2{qi}",))
                em.dma("sp", f"dgd{qi}", lambda h, t=tc, b=gcd[qi]: h.dma_start(
                    b[:, :], gcs[:, t * 1024:(t + 1) * 1024]),
                    r=("gcs_allA", "gcs_allB"), w=(f"gcd{qi}",))
                for half in range(2):
                    def wog(h, t=tc, half=half, b=gcd[qi], pWt=pW[pi]):
                        for kc in range(NKC):
                            h.matmul(pWt[:, half * 512:(half + 1) * 512],
                                     b[:, kc * 128:(kc + 1) * 128],
                                     wo_sb[:, kc * 1024 + half * 512: kc * 1024 + (half + 1) * 512],
                                     start=(kc == 0), stop=False)
                        return h.matmul(pWt[:, half * 512:(half + 1) * 512], ones16[0:1, 0:128],
                                        borow[0:1, half * 512:(half + 1) * 512],
                                        start=False, stop=True)
                    em.op("pe", wog, r=(f"gcd{qi}", "wo_sb", "consts", "ones16"), w=(f"pW{pi}h{half}",))
                    em.op("dve", lambda h, half=half, pWt=pW[pi], x=xl2[qi], o=ostg[pi]:
                          h.tensor_tensor(o[:, half * 512:(half + 1) * 512],
                                          pWt[:, half * 512:(half + 1) * 512],
                                          x[:, half * 512:(half + 1) * 512], ALU.add),
                          r=(f"pW{pi}h{half}", f"xl2{qi}"), w=(f"ostg{pi}",))
                em.dma("act", f"dow{pi}", lambda h, t=tc, b=ostg[pi]: h.dma_start(
                    out[t * 128:(t + 1) * 128, :], b[:, :]), r=(f"ostg{pi}",), w=(f"outp{tc}",))
                emit_gate_tc(tc)
            for tc in range(NTC):
                em.op("pe", lambda h, t=tc: h.transpose(ptg[0:8, (t % 8) * 128:(t % 8 + 1) * 128],
                                                        g_tm[:, t * 8:(t + 1) * 8], ident16[:, :]),
                      r=("g_tm", "consts"), w=("ptg",) if tc % 8 == 0 else ("ptgx",))
                if tc % 8 == 7:
                    em.op("act", lambda h, t=tc: h.copy(
                        gT8[0:8, (t - 7) * 128:(t + 1) * 128], ptg[0:8, :]),
                        r=("ptg", "ptgx"), w=("gT8",))
            em.mark("outp_allA", "dow0")
            em.mark("outp_allB", "dow1")
        em.barrier()
        if phase_limit < 5:
            em.flush(); _final_waits(em); return nc

        with ExitStack() as ph:
            psb = lambda name, shape, dt: ph.enter_context(nc.sbuf_tensor(name, list(shape), dt))
            pps = lambda name, shape, dt: ph.enter_context(nc.psum_tensor(name, list(shape), dt))
            yacc = psb("yacc", [128, 8 * 1024], F32)
            gh = [psb(f"gh{i}", [128, 4 * 1024], BF16) for i in range(2)]
            w1s = [psb(f"w1s{i}", [128, 4096], BF16) for i in range(4)]
            w2s = [psb(f"w2s{i}", [128, 4096], BF16) for i in range(2)]
            gbc = [psb(f"gbc{i}", [128, 1024], BF16) for i in range(8)]
            ghr = [psb(f"ghr{i}", [128, 512], BF16) for i in range(2)]
            pH = [pps(f"pH{i}", [128, 512], F32) for i in range(2)]
            pY = [pps(f"pY{i}", [128, 1024], F32) for i in range(2)]
            pBC = pps("pBC", [128, 512], F32)

            for pp in range(NPASS):
                T0 = pp * PASS
                em.dma("sp", "dyi", lambda h, T0=T0: h.dma_start(
                    yacc[:, :].rearrange("p (c o) -> p c o", c=8), ap_out_slice(out, T0)),
                    r=("outp_allA", "outp_allB") + tuple(f"yacc{i}" for i in range(8)),
                    w=tuple(f"yacc{i}" for i in range(8)))
                for e in range(8):
                    for half in range(2):
                        em.op("pe", lambda h, e=e, half=half, T0=T0: h.matmul(
                            pBC[:, :], sel8[0:8, e * 128:(e + 1) * 128],
                            gT8[0:8, T0 + half * 512: T0 + (half + 1) * 512],
                            start=True, stop=True),
                            r=("gT8", "consts"), w=("pBC",))
                        em.op("act", lambda h, e=e, half=half: h.copy(
                            gbc[e][:, half * 512:(half + 1) * 512], pBC[:, :]),
                            r=("pBC",), w=(f"gbc{e}",))
                for hc in range(NHC):
                    for eg in range(4):
                        for e4 in range(2):
                            e = eg * 2 + e4
                            wi = (hc * 8 + e) % 4
                            em.dma("pool", f"dw1{wi}", lambda h, e=e, hc=hc, b=w1s[wi]: h.dma_start(
                                b[:, :].rearrange("p (c m) -> p c m", c=8),
                                ap["w1"][e][:, hc * 512:(hc + 1) * 512].rearrange(
                                    "(c p) m -> p c m", p=128)), w=(f"w1s{wi}",))
                            em.dma("pool", f"dw2{e4}", lambda h, e=e, hc=hc, b=w2s[e4]: h.dma_start(
                                b[:, :].rearrange("p (c o) -> p c o", c=4),
                                ap["w2"][e][hc * 512:(hc + 1) * 512, :].rearrange(
                                    "(c p) o -> p c o", p=128)), w=(f"w2s{e4}",))
                            for m in range(4):
                                for nt in range(2):
                                    hi = (m * 2 + nt) & 1
                                    def mm1(h, m=m, nt=nt, T0=T0, b=w1s[wi], pHt=pH[hi]):
                                        last = None
                                        for kc in range(NKC):
                                            last = h.matmul(
                                                pHt[:, :],
                                                b[:, kc * 512 + m * 128: kc * 512 + (m + 1) * 128],
                                                xnT[:, kc * S + T0 + nt * 512: kc * S + T0 + (nt + 1) * 512],
                                                start=(kc == 0), stop=(kc == NKC - 1))
                                        return last
                                    em.op("pe", mm1, r=(f"w1s{wi}", "xnT"), w=(f"pH{hi}",))
                                    bidx = (hc * 4 + m) * 8 + e
                                    em.op("act", lambda h, pHt=pH[hi], o=ghr[hi], bi=bidx: h.activation(
                                        o[:, :], pHt[:, :], AF.Relu, bias=b1T[:, bi:bi + 1]),
                                        r=(f"pH{hi}", "consts"), w=(f"ghr{hi}",))
                                    em.op("dve", lambda h, o=gh[e4], m=m, nt=nt, i=ghr[hi], e=e:
                                          h.tensor_tensor(
                                              o[:, m * 1024 + nt * 512: m * 1024 + (nt + 1) * 512],
                                              i[:, :], gbc[e][:, nt * 512:(nt + 1) * 512], ALU.mult),
                                          r=(f"ghr{hi}", f"gbc{e}"), w=(f"gh{e4}",))
                        for tc in range(8):
                            yi = tc & 1
                            for half in range(2):
                                def mm2a(h, tc=tc, half=half, hc=hc, eg=eg, T0=T0, pYt=pY[yi]):
                                    first = True
                                    last = None
                                    if hc == 0 and eg == 0:
                                        last = h.matmul(pYt[:, half * 512:(half + 1) * 512],
                                                        gT8[0:8, T0 + tc * 128: T0 + (tc + 1) * 128],
                                                        b2sb[0:8, half * 512:(half + 1) * 512],
                                                        start=True, stop=False)
                                        first = False
                                    for m in range(4):
                                        last = h.matmul(
                                            pYt[:, half * 512:(half + 1) * 512],
                                            gh[0][:, m * 1024 + tc * 128: m * 1024 + tc * 128 + 128],
                                            w2s[0][:, m * 1024 + half * 512: m * 1024 + (half + 1) * 512],
                                            start=first, stop=False)
                                        first = False
                                    return last
                                def mm2b(h, tc=tc, half=half, pYt=pY[yi]):
                                    last = None
                                    for m in range(4):
                                        last = h.matmul(
                                            pYt[:, half * 512:(half + 1) * 512],
                                            gh[1][:, m * 1024 + tc * 128: m * 1024 + tc * 128 + 128],
                                            w2s[1][:, m * 1024 + half * 512: m * 1024 + (half + 1) * 512],
                                            start=False, stop=(m == 3))
                                    return last
                                em.op("pe", mm2a,
                                      r=("gh0", "w2s0", "gT8", "consts"), w=(f"pY{yi}h{half}",))
                                em.op("pe", mm2b,
                                      r=("gh1", "w2s1"), w=(f"pY{yi}h{half}",))
                            em.op("dve", lambda h, tc=tc, pYt=pY[yi]: h.tensor_tensor(
                                yacc[:, tc * 1024:(tc + 1) * 1024], pYt[:, :],
                                yacc[:, tc * 1024:(tc + 1) * 1024], ALU.add),
                                r=(f"pY{yi}h0", f"pY{yi}h1", f"yacc{tc}"), w=(f"yacc{tc}",))
                            if hc == NHC - 1 and eg == 3:
                                em.dma("sp", "dpw", lambda h, T0=T0, tc=tc: h.dma_start(
                                    out[T0 + tc * 128:T0 + (tc + 1) * 128, :],
                                    yacc[:, tc * 1024:(tc + 1) * 1024]),
                                    r=(f"yacc{tc}",), w=("yacc_st",))
        em.flush()
        _final_waits(em)
    return nc


def ap_out_slice(out, T0):
    return out[T0:T0 + PASS, :].rearrange("(c p) o -> p c o", p=128)


def _final_waits(em):
    with em.nc.Block() as blk:
        sems = [(em.sem[n], em.cnt[n]) for n in em.sem
                if n.startswith(("dow", "dpw", "ddbg", "dgc")) and em.cnt[n] > 0]
        def _(sy):
            for s, v in sems:
                sy.wait_ge(s, v)
        blk.sync(_)


def _prep(inputs):
    c = make_consts()
    W = inputs
    base = dict(
        wq=W["L0_attn_Wq"], wk=W["L0_attn_Wk"], wv=W["L0_attn_Wv"],
        wg=W["L0_attn_Wg"], wo=W["L0_attn_Wo"],
        w1=W["L0_ffn_w1"], w2=W["L0_ffn_w2"],
        bvrow=np.ascontiguousarray(np.asarray(W["L0_attn_bv"]).reshape(1, D)).astype(BF),
        borow=np.ascontiguousarray(np.asarray(W["L0_attn_bo"]).reshape(1, D)).astype(BF),
        gbrow=np.ascontiguousarray(np.asarray(W["L0_ffn_gate_b"]).reshape(1, 8)).astype(BF),
        bT=np.ascontiguousarray(
            np.stack([W["L0_attn_bq"], W["L0_attn_bk"], W["L0_attn_bg"]])
            .reshape(3, 8, 128).transpose(2, 1, 0).reshape(128, 24)).astype(np.float32),
        b1T=np.ascontiguousarray(
            np.asarray(W["L0_ffn_b1"]).reshape(8, 32, 128).transpose(2, 1, 0)
            .reshape(128, 256)).astype(np.float32),
        b2sb=np.ascontiguousarray(W["L0_ffn_b2"]).astype(BF),
        gnw2=np.ascontiguousarray(np.asarray(W["L0_attn_gn_w"]).reshape(8, 2).T).astype(np.float32),
        gnb2=np.ascontiguousarray(np.asarray(W["L0_attn_gn_b"]).reshape(8, 2).T).astype(np.float32),
        gate_wT=np.ascontiguousarray(np.asarray(W["L0_ffn_gate_w"]).T).astype(np.float32),
        tqa=c["tqa"], tqb=c["tqb"], tka=c["tka"], tkb=c["tkb"],
        maskbig=c["maskbig"], ident16=c["ident16"], shf=c["shf"],
        e2=c["e2"], p2=c["p2"], sel8=c["sel8"], gamC=c["gamC"],
    )
    for k in ("wq", "wk", "wv", "wg", "wo", "w1", "w2"):
        base[k] = np.ascontiguousarray(np.asarray(base[k], np.float32))
    return base


def kernel(**inputs):
    base = _prep(inputs)
    nc = build()
    x = np.asarray(inputs["L0_attn_x"], np.float32)
    in_maps = [dict(base, x=np.ascontiguousarray(x[cid])) for cid in range(B)]
    res = run_bass_kernel_spmd(nc, in_maps, core_ids=list(range(B)))
    return np.stack([np.asarray(res.results[cid]["out"]) for cid in range(B)])


# version 22
# speedup vs baseline: 1.1788x; 1.1788x over previous
import numpy as np
import ml_dtypes
from contextlib import ExitStack

import concourse.bass as bass
import concourse.mybir as mybir
from concourse.bass_utils import run_bass_kernel_spmd

F32 = mybir.dt.float32
BF16 = mybir.dt.bfloat16
AF = mybir.ActivationFunctionType
ALU = mybir.AluOpType
AX = mybir.AxisListType
BF = ml_dtypes.bfloat16

B, S, D = 8, 4096, 1024
SWAPMASK = [i + 1 - 2 * (i % 2) for i in range(32)]
H, DH, ROT = 16, 64, 32
NE, HID = 8, 4096
NPAIR, NKC, NTC, NIT = 8, 8, 32, 8
PASS, NPASS = 1024, 4
HCW, NHC = 512, 8
GN_N = float(64 * S)


def _gammas():
    return 1.0 - 2.0 ** (-5.0 - np.arange(H, dtype=np.float64))


def make_tables():
    t = np.arange(S, dtype=np.float64)
    inv_freq = 1.0 / (10000.0 ** (np.arange(0, ROT, 2, dtype=np.float64) / ROT))
    freqs = np.repeat(t[:, None] * inv_freq[None, :], 2, axis=-1)
    base = (np.arange(0, ROT, 2, dtype=np.float64) + 0.4 * ROT) / (1.4 * ROT)
    power = (t - S // 2) / 512.0
    scale = np.repeat(base[None, :] ** power[:, None], 2, axis=-1)
    cosv, sinv = np.cos(freqs), np.sin(freqs)
    gamma = _gammas()
    sh = (1.0 - gamma[None, :] ** (t[:, None] + 1)) / (1.0 - gamma[None, :])
    cfac = gamma[None, :] ** (t[:, None] % 512) / np.sqrt(sh)
    bfac = gamma[None, :] ** (-(t[:, None] % 128))
    bsign = np.where(np.arange(ROT) % 2 == 0, -1.0, 1.0)
    TQA = np.zeros((NPAIR, 128, S)); TQB = np.zeros((NPAIR, 128, S))
    TKA = np.zeros((NPAIR, 128, S)); TKB = np.zeros((NPAIR, 128, S))
    A = np.ones((S, 64)); Bq = np.zeros((S, 64))
    A[:, :ROT] = cosv * scale
    Bq[:, :ROT] = bsign[None, :] * (sinv * scale)
    Ak = np.ones((S, 64)); Bk = np.zeros((S, 64))
    Ak[:, :ROT] = cosv / scale
    Bk[:, :ROT] = bsign[None, :] * (sinv / scale)
    for hp in range(NPAIR):
        for half in range(2):
            hh = 2 * hp + half
            sl = slice(half * 64, half * 64 + 64)
            TQA[hp, sl, :] = (A * cfac[:, hh:hh + 1]).T
            TQB[hp, sl, :] = (Bq * cfac[:, hh:hh + 1]).T
            TKA[hp, sl, :] = (Ak * bfac[:, hh:hh + 1]).T
            TKB[hp, sl, :] = (Bk * bfac[:, hh:hh + 1]).T
    return [x.astype(BF) for x in (TQA, TQB, TKA, TKB)]


def make_consts():
    c = {}
    c["tqa"], c["tqb"], c["tka"], c["tkb"] = make_tables()
    m = np.zeros((128, 896), np.float32)
    jj = np.arange(128)[:, None]; cc = np.arange(128)[None, :]
    m[:, 384:512] = (cc >= jj).astype(np.float32)
    m[:, 512:] = 1.0
    c["maskbig"] = m
    c["ident16"] = np.eye(128, dtype=np.float32).astype(BF)
    shf = np.zeros((128, 128), np.float32)
    for i in range(0, 128, 2):
        shf[i + 1, i] = 1.0
        shf[i, i + 1] = 1.0
    c["shf"] = shf.astype(BF)
    e2 = np.zeros((128, 2), np.float32)
    e2[:64, 0] = 1.0; e2[64:, 1] = 1.0
    c["e2"] = e2
    p2 = np.zeros((2, 128), np.float32)
    p2[0, :64] = 1.0; p2[1, 64:] = 1.0
    c["p2"] = p2
    sel8 = np.zeros((8, 8 * 128), np.float32)
    for e in range(8):
        sel8[e, e * 128:(e + 1) * 128] = 1.0
    c["sel8"] = sel8.astype(BF)
    gamma = _gammas()
    gc512 = np.zeros((128, 8), np.float32)
    for hp in range(8):
        gc512[:64, hp] = gamma[2 * hp] ** 512.0
        gc512[64:, hp] = gamma[2 * hp + 1] ** 512.0
    c["gamC"] = gc512
    return c


class EM:
    ENGS = ("pe", "dve", "act", "sp", "pool")

    def __init__(self, nc, st):
        self.nc = nc
        self.st = st
        self.q = {e: [] for e in self.ENGS}
        self.sem = {}
        self.cnt = {}
        self.waited = {e: {} for e in self.ENGS}
        self.resw = {}
        self.resr = {}
        for n in ("pe", "dve", "act"):
            self._mksem(n)

    def _mksem(self, name):
        self.sem[name] = self.st.enter_context(self.nc.semaphore(name))
        self.cnt[name] = 0

    def _need(self, reads, writes):
        need = {}
        def add(s, v):
            if v > need.get(s, 0):
                need[s] = v
        for r in reads:
            w = self.resw.get(r)
            if w:
                add(*w)
        for r in writes:
            w = self.resw.get(r)
            if w:
                add(*w)
            for s, v in self.resr.get(r, {}).items():
                add(s, v)
        return need

    def _waits(self, eng, need):
        for s in sorted(need):
            v = need[s]
            if eng == "pe" and s == "pe":
                continue
            if v <= self.waited[eng].get(s, 0):
                continue
            self.waited[eng][s] = v
            sem = self.sem[s]
            self.q[eng].append(lambda h, sem=sem, v=v: h.wait_ge(sem, v))

    def op(self, eng, fn, r=(), w=()):
        self._waits(eng, self._need(r, w))
        self.cnt[eng] += 1
        v = self.cnt[eng]
        sem = self.sem[eng]
        self.q[eng].append(lambda h, fn=fn, sem=sem: fn(h).then_inc(sem, 1))
        for x in w:
            self.resw[x] = (eng, v)
            self.resr[x] = {}
        for x in r:
            self.resr.setdefault(x, {})[eng] = v

    def dma(self, queue, semname, fn, r=(), w=()):
        self.dma_group(queue, semname, [(fn, r, w)])

    def dma_group(self, queue, semname, items):
        if semname not in self.sem:
            self._mksem(semname)
        allw = []
        for fn, r, w in items:
            self._waits(queue, self._need(r, w))
            self.cnt[semname] += 16
            sem = self.sem[semname]
            self.q[queue].append(lambda h, fn=fn, sem=sem: fn(h).then_inc(sem, 16))
            allw.extend(w)
            for x in r:
                self.resr.setdefault(x, {})[semname] = self.cnt[semname]
        v = self.cnt[semname]
        for x in allw:
            self.resw[x] = (semname, v)
            self.resr[x] = {}

    def mark(self, res, semname):
        self.resw[res] = (semname, self.cnt.get(semname, 0))
        self.resr[res] = {}

    def flush(self):
        if not any(self.q.values()):
            return
        with self.nc.Block() as blk:
            m = {"pe": blk.tensor, "dve": blk.vector, "act": blk.scalar,
                 "sp": blk.sync, "pool": blk.gpsimd}
            for e in self.ENGS:
                lst = self.q[e]
                if lst:
                    m[e](lambda h, lst=tuple(lst): [f(h) for f in lst])
        self.q = {e: [] for e in self.ENGS}

    def barrier(self):
        self.flush()
        for e in self.ENGS:
            for s in ("pe", "dve", "act"):
                self.waited[e][s] = self.cnt[s]


def build(phase_limit=99, debug=False):
    nc = bass.Bass("TRN2", target_bir_lowering=False)
    ap = {}
    def din(name, shape, dt=F32):
        ap[name] = nc.dram_tensor(name, list(shape), dt, kind="ExternalInput").ap()
    din("x", (S, D))
    for n in ("wq", "wk", "wv", "wg", "wo"):
        din(n, (D, D))
    din("bvrow", (1, D), BF16)
    din("borow", (1, D), BF16)
    din("gbrow", (1, 8), BF16)
    din("bT", (128, 24))
    din("b1T", (128, 32 * 8))
    din("b2sb", (8, D), BF16)
    din("gnw2", (2, 8)); din("gnb2", (2, 8))
    din("gate_wT", (D, 8))
    din("w1", (NE, D, HID)); din("w2", (NE, HID, D))
    for n in ("tqa", "tqb", "tka", "tkb"):
        din(n, (NPAIR, 128, S), BF16)
    din("maskbig", (128, 896))
    din("ident16", (128, 128), BF16)
    din("shf", (128, 128), BF16)
    din("e2", (128, 2)); din("p2", (2, 128))
    din("sel8", (8, 8 * 128), BF16)
    din("gamC", (128, 8))
    out = nc.dram_tensor("out", [S, D], F32, kind="ExternalOutput").ap()
    gcs = nc.dram_tensor("gcs", [128, NPAIR * S], BF16, kind="Internal").ap()
    xns = nc.dram_tensor("xns", [S, D], BF16, kind="Internal").ap()
    dbg = {}
    if debug:
        for n, shape, dt in (("d_xnT", [128, NKC * S], BF16), ("d_gtm", [128, 256], BF16),
                             ("d_gt8", [8, S], BF16), ("d_gcs", [128, NPAIR * S], BF16)):
            dbg[n] = nc.dram_tensor(n, shape, dt, kind="ExternalOutput").ap()

    gam = _gammas()

    st = ExitStack()
    with st:
        em = EM(nc, st)
        sb = lambda name, shape, dt: st.enter_context(nc.sbuf_tensor(name, list(shape), dt))

        xnT = sb("xnT", [128, NKC * S], BF16)
        ident16 = sb("c_id16", [128, 128], BF16)
        e2 = sb("c_e2", [128, 2], F32)
        p2 = sb("c_p2", [2, 128], F32)
        sel8 = sb("c_sel8", [8, 8 * 128], BF16)
        maskbig = sb("c_mask", [128, 896], F32)
        bT = sb("c_bT", [128, 24], F32)
        b1T = sb("c_b1T", [128, 256], F32)
        b2sb = sb("c_b2", [8, D], BF16)
        bvrow = sb("c_bv", [1, D], BF16)
        borow = sb("c_bo", [1, D], BF16)
        gbrow = sb("c_gb", [1, 8], BF16)
        gnw2 = sb("c_gnw", [2, 8], F32)
        gnb2 = sb("c_gnb", [2, 8], F32)
        gwT = sb("c_gwT", [128, NKC * 8], BF16)
        ones16 = sb("c_ones", [1, 512], BF16)
        epsT = sb("c_eps", [128, 1], F32)
        g_tm = sb("g_tm", [128, NTC * 8], BF16)
        gT8 = sb("gT8", [8, S], BF16)
        gamC = sb("c_gamC", [128, 8], F32)

        citems = []
        for dst, src in ((maskbig, "maskbig"), (e2, "e2"), (p2, "p2"), (bT, "bT"),
                         (b1T, "b1T"), (gnw2, "gnw2"), (gnb2, "gnb2"),
                         (ident16, "ident16"), (sel8, "sel8"),
                         (b2sb, "b2sb"), (bvrow, "bvrow"), (borow, "borow"),
                         (gbrow, "gbrow"), (gamC, "gamC")):
            citems.append((lambda h, d=dst, s=ap[src]: h.dma_start(d[:, :], s[:, :]), (), ()))
        em.dma_group("sp", "dcs", citems)
        em.mark("consts", "dcs")
        em.dma("pool", "dcsg", lambda h: h.dma_start(
            gwT[:, :].rearrange("p (c e) -> p c e", c=8),
            ap["gate_wT"].rearrange("(c p) e -> p c e", p=128),
            allow_slow_non_contiguous=True), w=("gwT",))
        em.op("dve", lambda h: h.memset(ones16[:, :], 1.0), w=("ones16",))
        em.op("dve", lambda h: h.memset(epsT[:, :], 1e-5), w=("epsT",))

        with ExitStack() as ph:
            psb = lambda name, shape, dt: ph.enter_context(nc.sbuf_tensor(name, list(shape), dt))
            pps = lambda name, shape, dt: ph.enter_context(nc.psum_tensor(name, list(shape), dt))
            xld = [psb(f"xld{i}", [128, D], F32) for i in range(4)]
            xn_tm = [psb(f"xntm{i}", [128, D], BF16) for i in range(2)]
            lnstb = [psb(f"lnst{i}", [128, 8], F32) for i in range(3)]
            lnbb = [psb(f"lnb{i}", [128, 12], F32) for i in range(3)]

            def ln_stage1(tc):
                pi = tc % 4
                t3 = tc % 3
                lnst = lnstb[t3]
                LS = f"_{t3}"
                xb = f"xld{pi}"
                em.dma("sp", f"dx{pi}",
                       lambda h, t=tc, b=xld[pi]: h.dma_start(b[:, :], ap["x"][t * 128:(t + 1) * 128, :]),
                       w=(xb,))
                lnb = lnbb[t3]
                em.op("dve", lambda h, b=xld[pi], o=lnb: h.bn_stats(o[:, 0:6], b[:, 0:512]),
                      r=(xb,), w=("lnb" + LS,))
                em.op("dve", lambda h, b=xld[pi], o=lnb: h.bn_stats(o[:, 6:12], b[:, 512:1024]),
                      r=(xb, "lnb" + LS), w=("lnb" + LS,))
                em.op("dve", lambda h, i=lnb: h.bn_aggr(lnst[:, 2:4], i[:, 0:12]),
                      r=("lnb" + LS,), w=("lnst2" + LS, "lnst3" + LS))
                em.op("act", lambda h: h.activation(lnst[:, 4:5], lnst[:, 3:4], AF.Ln, bias=epsT[:, 0:1]),
                      r=("lnst3" + LS, "epsT"), w=("lnst4" + LS,))
                em.op("act", lambda h: h.activation(lnst[:, 5:6], lnst[:, 4:5], AF.Exp, scale=-0.5),
                      r=("lnst4" + LS,), w=("lnst5" + LS,))
            def ln_stage2(tc):
                pi = tc % 4
                xi = tc & 1
                t3 = tc % 3
                lnst = lnstb[t3]
                LS = f"_{t3}"
                xb = f"xld{pi}"
                em.op("dve", lambda h, b=xld[pi], o=xn_tm[xi]: h.tensor_scalar(
                    o[:, :], b[:, :], lnst[:, 2:3], lnst[:, 5:6], ALU.subtract, ALU.mult),
                    r=(xb, "lnst2" + LS, "lnst5" + LS), w=(f"xntm{xi}",))

                em.dma("act", f"dxn{xi}", lambda h, t=tc, b=xn_tm[xi]: h.dma_start(
                    xns[t * 128:(t + 1) * 128, :], b[:, :]), r=(f"xntm{xi}",), w=(f"xnsw{tc}",))
            for tc in range(NTC):
                ln_stage1(tc)
                if tc > 1:
                    ln_stage2(tc - 2)
            ln_stage2(NTC - 2)
            ln_stage2(NTC - 1)
            for kc in range(NKC):
                em.dma("sp", "dxt", lambda h, kc=kc: h.dma_start(
                    xnT[:, kc * S:(kc + 1) * S], xns[:, kc * 128:(kc + 1) * 128], transpose=True),
                    r=tuple(f"xnsw{t}" for t in range(NTC)), w=("xnT",))
        em.barrier()
        if debug:
            em.dma("sp", "ddbg", lambda h: h.dma_start(dbg["d_xnT"][:, :], xnT[:, :]), r=("xnT",))
        if phase_limit < 2:
            em.flush(); _final_waits(em); return nc

        em.barrier()
        if phase_limit < 3:
            em.flush(); _final_waits(em); return nc

        with ExitStack() as ph:
            psb = lambda name, shape, dt: ph.enter_context(nc.sbuf_tensor(name, list(shape), dt))
            pps = lambda name, shape, dt: ph.enter_context(nc.psum_tensor(name, list(shape), dt))
            qT = [psb(f"qT{i}", [128, S], BF16) for i in range(2)]
            kT = [psb(f"kT{i}", [128, S], BF16) for i in range(2)]
            vtm = [psb(f"v{i}", [128, S], BF16) for i in range(2)]
            q_raw = {(wq, i): psb(f"qraw{wq}{i}", [128, 512], BF16)
                     for wq in ("q", "k") for i in range(2)}
            t1b = [psb(f"t1_{i}", [128, 512], BF16) for i in range(2)]
            t2b = [psb(f"t2_{i}", [128, 512], BF16) for i in range(2)]
            tabbuf = [psb(f"tab{i}", [128, 2048], BF16) for i in range(4)]
            s_tile = [psb(f"stile{i}", [128, 512], BF16) for i in range(4)]
            g_tile = [psb(f"gtile{i}", [128, 512], BF16) for i in range(2)]
            gcc = [psb(f"gcc{i}", [128, S], BF16) for i in range(2)]
            wsl = {n: [psb(f"w{n}{i}", [128, 1024], BF16) for i in range(2)]
                   for n in ("q", "k", "v", "g")}
            gnst = psb("gnst", [128, 12], F32)
            gnb = psb("gnb", [128, 48], F32)
            gnt = psb("gnt", [2, 8], F32)
            ab_bc = psb("abbc", [128, 2], F32)
            pA = pps("pA", [128, 512], F32)
            pS = pps("pS", [128, 512], F32)
            qs_sb = [psb(f"qs{i}", [128, 512], BF16) for i in range(2)]
            pC = [pps(f"pC{i}", [128, 512], F32) for i in range(2)]
            pR = [pps(f"pR{i}", [128, 512], F32) for i in range(2)]
            pST = pps("pST", [128, 512], F32)
            pTb = pps("pTb", [128, 1024], BF16)
            S32 = psb("S32", [128, 64], F32)
            S16 = psb("S16", [128, 64], BF16)
            ktm = [psb(f"ktm{i}", [128, 512], BF16) for i in range(2)]
            pB = pA

            WMAP = {"q": "wq", "k": "wk", "v": "wv", "g": "wg"}

            def load_pair_w(p):
                pi = p & 1
                items = []
                for n in ("q", "k", "v", "g"):
                    src = ap[WMAP[n]][:, p * 128:(p + 1) * 128].rearrange(
                        "(c p2) m -> p2 c m", p2=128)
                    items.append((lambda h, d=wsl[n][pi], s=src: h.dma_start(
                        d[:, :].rearrange("p (c m) -> p c m", c=8), s),
                                  (), (f"w{n}{pi}",)))
                em.dma_group("pool", f"dw{pi}", items)

            def emit_proj(p):
                pi = p & 1
                load_pair_w(p)
                for it in range(NIT):
                    ti = (p * NIT + it) % 4
                    em.dma_group("sp", f"dt{ti}", [
                        (lambda h, d=tabbuf[ti], s=ap[tn][p][:, it * 512:(it + 1) * 512], j=j:
                         h.dma_start(d[:, j * 512:(j + 1) * 512], s), (), (f"tab{ti}",))
                        for j, tn in enumerate(("tqa", "tqb", "tka", "tkb"))])
                    for which, tcol in (("q", 0), ("k", 2)):
                        dstT = (qT if which == "q" else kT)[pi]
                        wbuf = wsl[which][pi]
                        wres = f"w{which}{pi}"
                        bcol = p * 3 + (0 if which == "q" else 1)
                        pP = pA if (it & 1) == 0 else pS
                        pPr = "pA" if (it & 1) == 0 else "pS"
                        qi = it & 1
                        qr = q_raw[(which, qi)]
                        qres = f"qraw{which}{qi}"
                        def proj(h, w=wbuf, it=it, pP=pP):
                            last = None
                            for kc in range(NKC):
                                last = h.matmul(pP[:, :], w[:, kc * 128:(kc + 1) * 128],
                                                xnT[:, kc * S + it * 512: kc * S + (it + 1) * 512],
                                                start=(kc == 0), stop=(kc == NKC - 1))
                            return last
                        em.op("pe", proj, r=(wres, "xnT"), w=(pPr,))
                        em.op("act", lambda h, pP=pP, o=qr, b=bcol: h.activation(
                            o[:, :], pP[:, :], AF.Identity, bias=bT[:, b:b + 1]),
                            r=(pPr, "consts"), w=(qres,))
                        em.op("dve", lambda h, i=qr, o=qs_sb[qi]: h.stream_shuffle(
                            o[:, :], i[:, :], SWAPMASK),
                            r=(qres,), w=(f"qs{qi}",))
                        em.op("dve", lambda h, i=qr, tb=tabbuf[ti], tcol=tcol, t1=t1b[qi]: h.tensor_tensor(
                            t1[:, :], i[:, :], tb[:, tcol * 512:(tcol + 1) * 512], ALU.mult),
                            r=(qres, f"tab{ti}"), w=(f"t1_{qi}",))
                        em.op("dve", lambda h, o=qs_sb[qi], tb=tabbuf[ti], tcol=tcol, t2=t2b[qi]: h.tensor_tensor(
                            t2[:, :], o[:, :], tb[:, (tcol + 1) * 512:(tcol + 2) * 512], ALU.mult),
                            r=(f"qs{qi}", f"tab{ti}"), w=(f"t2_{qi}",))
                        em.op("dve", lambda h, o=dstT, it=it, t1=t1b[qi], t2=t2b[qi]: h.tensor_tensor(
                            o[:, it * 512:(it + 1) * 512], t1[:, :], t2[:, :], ALU.add),
                            r=(f"t1_{qi}", f"t2_{qi}"), w=(f"{which}T{pi}",))
                for tc in range(NTC):
                    pP = pA
                    pPr = "pA"
                    sl = (tc % 4) * 128
                    def vproj(h, tc=tc, pP=pP, sl=sl, w=wsl["v"][pi], p=p):
                        for kc in range(NKC):
                            h.matmul(pP[:, sl:sl + 128],
                                     xnT[:, kc * S + tc * 128: kc * S + tc * 128 + 128],
                                     w[:, kc * 128:(kc + 1) * 128], start=(kc == 0), stop=False)
                        return h.matmul(pP[:, sl:sl + 128], ones16[0:1, 0:128],
                                        bvrow[0:1, p * 128:(p + 1) * 128], start=False, stop=True)
                    em.op("pe", vproj, r=(f"wv{pi}", "xnT", "consts", "ones16"),
                          w=((pPr,) if tc % 4 == 0 else (pPr + "x",)))
                    if tc % 4 == 3:
                        em.op("act", lambda h, tc=tc, pP=pP, o=vtm[pi]: h.copy(
                            o[:, (tc - 3) * 128:(tc + 1) * 128], pP[:, :]),
                            r=(pPr, pPr + "x"), w=(f"v{pi}",))

            def emit_retention(p):
                pi = p & 1
                em.op("dve", lambda h: h.memset(S32[:, :], 0.0), w=("S32",))
                em.op("dve", lambda h: h.memset(S16[:, :], 0.0), w=("S16",))
                steps = [(c, half, jc) for c in range(NIT) for half in range(2)
                         for jc in range(4)]
                LOOK = 4

                def emit_mm1_evac(s):
                    c, half, jc, = steps[s]
                    ci = s % 2
                    g = gam[2 * p + half]
                    def mm1(h, half=half, c=c, jc=jc, pCt=pC[ci]):
                        return h.matmul(pCt[:, :],
                                        kT[pi][half * 64:half * 64 + 64,
                                               c * 512 + jc * 128: c * 512 + (jc + 1) * 128],
                                        qT[pi][half * 64:half * 64 + 64, c * 512:(c + 1) * 512],
                                        start=True, stop=True)
                    em.op("pe", mm1, r=(f"kT{pi}", f"qT{pi}"), w=(f"pC{ci}",))
                    scal = float(np.float32(g ** float(-128 * jc)))
                    em.op("dve", lambda h, pCt=pC[ci], o=s_tile[jc], jc=jc, scal=scal:
                          h.scalar_tensor_tensor(o[:, jc * 128:512], pCt[:, jc * 128:512], scal,
                                                 maskbig[:, 384:896 - jc * 128],
                                                 ALU.mult, ALU.mult),
                          r=(f"pC{ci}", "consts"), w=(f"stile{jc}",))

                def emit_mm2(s):
                    c, half, jc = steps[s]
                    ri = c & 1
                    rs = half * 64
                    if jc == 0:
                        if c > 0:
                            def cross(h, half=half, c=c, pRt=pR[ri], rs=rs):
                                return h.matmul(pRt[rs:rs + 64, :],
                                                S16[rs:rs + 64, :],
                                                qT[pi][rs:rs + 64, c * 512:(c + 1) * 512],
                                                start=True, stop=False)
                            em.op("pe", cross, r=("S16", f"qT{pi}"), w=(f"pR{ri}",))
                    def mm2(h, half=half, c=c, jc=jc, st=s_tile[jc], pRt=pR[ri], rs=rs,
                            first=(jc == 0 and c == 0), last=(jc == 3)):
                        return h.matmul(pRt[rs:rs + 64, :],
                                        vtm[pi][:, (c * 4 + jc) * 128 + half * 64:
                                                (c * 4 + jc) * 128 + half * 64 + 64],
                                        st[:, :], start=first, stop=last)
                    em.op("pe", mm2, r=(f"v{pi}", f"stile{jc}", "S16" if jc == 0 else f"v{pi}"),
                          w=(f"pR{ri}",))
                    if jc == 3 and half == 1:
                        em.op("act", lambda h, pRt=pR[ri], o=gcc[pi], c=c: h.copy(
                            o[:, c * 512:(c + 1) * 512], pRt[:, :]),
                            r=(f"pR{ri}",), w=(f"gcc{pi}",))

                def emit_state_prep(c):
                    ki = c & 1
                    kb = ktm[ki]
                    for sub in range(4):
                        em.op("pe", lambda h, c=c, sub=sub: h.transpose(
                            pTb[:, sub * 128:(sub + 1) * 128],
                            kT[pi][:, c * 512 + sub * 128: c * 512 + (sub + 1) * 128],
                            ident16[:, :]),
                            r=(f"kT{pi}", "consts"), w=("pTb",) if sub == 0 else ("pTbx",))
                    for sub in range(4):
                        for half in range(2):
                            g = gam[2 * p + half]
                            scal = float(np.float32(g ** float(512 - 128 * sub)))
                            em.op("act", lambda h, sub=sub, half=half, scal=scal, kb=kb: h.activation(
                                kb[:, sub * 128 + half * 64: sub * 128 + half * 64 + 64],
                                pTb[:, sub * 128 + half * 64: sub * 128 + half * 64 + 64],
                                AF.Copy, scale=scal),
                                r=("pTb", "pTbx"), w=(f"ktm{ki}",))

                def emit_state_upd(c):
                    ki = c & 1
                    kb = ktm[ki]
                    for half in range(2):
                        rs = half * 64
                        def smm(h, c=c, half=half, rs=rs, kb=kb):
                            last = None
                            for sub in range(4):
                                last = h.matmul(pST[rs:rs + 64, 0:64],
                                               kb[:, sub * 128 + rs: sub * 128 + rs + 64],
                                               vtm[pi][:, (c * 4 + sub) * 128 + rs:
                                                       (c * 4 + sub) * 128 + rs + 64],
                                               start=(sub == 0), stop=(sub == 3))
                            return last
                        em.op("pe", smm, r=(f"ktm{ki}", f"v{pi}", "pSTg"), w=(f"pSTh{half}",))
                    em.op("dve", lambda h, p=p: h.scalar_tensor_tensor(
                        S32[:, :], S32[:, :], gamC[:, p:p + 1], pST[:, 0:64],
                        ALU.mult, ALU.add),
                        r=("S32", "pSTh0", "pSTh1", "consts"), w=("S32",))
                    em.op("act", lambda h: h.copy(S16[:, :], S32[:, :]),
                          r=("S32",), w=("S16",))

                emit_state_prep(0)
                emit_state_prep(1)
                n = len(steps)
                for s in range(n + LOOK):
                    if s >= LOOK:
                        emit_mm2(s - LOOK)
                        c2, half2, jc2 = steps[s - LOOK]
                        if half2 == 1 and jc2 == 0:
                            emit_state_upd(c2)
                            if c2 + 2 < NIT:
                                emit_state_prep(c2 + 2)
                    if s < n:
                        emit_mm1_evac(s)

            def emit_gn_gate(p):
                pi = p & 1
                gres = f"gcc{pi}"
                gch = gcc[pi]
                for c8 in range(8):
                    em.op("dve", lambda h, gch=gch, c8=c8: h.bn_stats(
                        gnb[:, c8 * 6:(c8 + 1) * 6], gch[:, c8 * 512:(c8 + 1) * 512]),
                        r=(gres,) + (("gnb",) if c8 else ()), w=("gnb",))
                em.op("dve", lambda h: h.bn_aggr(gnst[:, 2:4], gnb[:, 0:48]),
                      r=("gnb",), w=("gnstmv",))
                em.op("dve", lambda h: h.tensor_scalar(gnst[:, 0:1], gnst[:, 2:3], 4096.0, None, ALU.mult),
                      r=("gnstmv",), w=("gnst0",))
                em.op("dve", lambda h: h.tensor_tensor(gnst[:, 4:5], gnst[:, 2:3], gnst[:, 2:3], ALU.mult),
                      r=("gnstmv",), w=("gnst4",))
                em.op("dve", lambda h: h.tensor_tensor(gnst[:, 4:5], gnst[:, 3:4], gnst[:, 4:5], ALU.add),
                      r=("gnstmv", "gnst4"), w=("gnst4",))
                em.op("dve", lambda h: h.tensor_scalar(gnst[:, 1:2], gnst[:, 4:5], 4096.0, None, ALU.mult),
                      r=("gnst4",), w=("gnst1",))
                em.op("pe", lambda h: h.matmul(pST[0:2, 64:66], e2[:, :], gnst[:, 0:2],
                                               start=True, stop=True),
                      r=("gnst0", "gnst1", "consts", "pSTh0", "pSTh1"), w=("pSTg",))
                em.op("dve", lambda h: h.tensor_copy(gnt[:, 0:2], pST[0:2, 64:66]),
                      r=("pSTg",), w=("gnt01",))
                em.op("dve", lambda h: h.tensor_scalar(gnt[:, 2:3], gnt[:, 0:1], 1.0 / GN_N, None, ALU.mult),
                      r=("gnt01",), w=("gnt2",))
                em.op("dve", lambda h: h.tensor_tensor(gnt[:, 3:4], gnt[:, 2:3], gnt[:, 2:3], ALU.mult),
                      r=("gnt2",), w=("gnt3",))
                em.op("dve", lambda h: h.scalar_tensor_tensor(gnt[:, 3:4], gnt[:, 1:2], 1.0 / GN_N,
                                                              gnt[:, 3:4], ALU.mult, ALU.subtract),
                      r=("gnt01", "gnt3"), w=("gnt3",))
                em.op("act", lambda h: h.activation(gnt[:, 4:5], gnt[:, 3:4], AF.Ln, bias=epsT[0:2, 0:1]),
                      r=("gnt3", "epsT"), w=("gnt4",))
                em.op("act", lambda h: h.activation(gnt[:, 5:6], gnt[:, 4:5], AF.Exp, scale=-0.5),
                      r=("gnt4",), w=("gnt5",))
                em.op("dve", lambda h, p=p: h.tensor_tensor(gnt[:, 6:7], gnt[:, 5:6],
                                                            gnw2[:, p:p + 1], ALU.mult),
                      r=("gnt5", "consts"), w=("gnt6",))
                em.op("dve", lambda h: h.tensor_tensor(gnt[:, 3:4], gnt[:, 2:3], gnt[:, 6:7], ALU.mult),
                      r=("gnt2", "gnt6"), w=("gnt3",))
                em.op("dve", lambda h, p=p: h.tensor_tensor(gnt[:, 7:8], gnb2[:, p:p + 1],
                                                            gnt[:, 3:4], ALU.subtract),
                      r=("gnt3", "consts"), w=("gnt7",))
                em.op("pe", lambda h: h.matmul(pST[:, 66:68], p2[:, :], gnt[:, 6:8],
                                               start=True, stop=True),
                      r=("gnt6", "gnt7", "consts", "pSTg"), w=("pSTg",))
                em.op("dve", lambda h: h.tensor_copy(ab_bc[:, :], pST[:, 66:68]),
                      r=("pSTg",), w=("ab_bc",))
                em.op("act", lambda h, gch=gch: h.activation(gch[:, :], gch[:, :], AF.Identity,
                                                             scale=ab_bc[:, 0:1], bias=ab_bc[:, 1:2]),
                      r=(gres, "ab_bc"), w=(gres,))
                for it in range(NIT):
                    gi = it & 1
                    pP = pA
                    pPr = "pA"
                    def gproj(h, it=it, pP=pP, w=wsl["g"][pi]):
                        last = None
                        for kc in range(NKC):
                            last = h.matmul(pP[:, :], w[:, kc * 128:(kc + 1) * 128],
                                            xnT[:, kc * S + it * 512: kc * S + (it + 1) * 512],
                                            start=(kc == 0), stop=(kc == NKC - 1))
                        return last
                    em.op("pe", gproj, r=(f"wg{pi}", "xnT"), w=(pPr,))
                    em.op("act", lambda h, pP=pP, o=g_tile[gi], b=p * 3 + 2: h.activation(
                        o[:, :], pP[:, :], AF.Silu, bias=bT[:, b:b + 1]),
                        r=(pPr, "consts"), w=(f"gtile{gi}",))
                    em.op("dve", lambda h, it=it, gt=g_tile[gi], gch=gch: h.tensor_tensor(
                        gch[:, it * 512:(it + 1) * 512], gch[:, it * 512:(it + 1) * 512],
                        gt[:, :], ALU.mult),
                        r=(gres, f"gtile{gi}"), w=(gres,))
                em.dma("pool", f"dgc{pi}", lambda h, p=p, gch=gch: h.dma_start(
                    bass.AP(gcs.tensor, p * 128, [[NPAIR * S, 128], [1024, 32], [1, 128]]),
                    gch[:, :].rearrange("q (c m) -> q c m", m=128)), r=(gres,))
                if debug:
                    em.dma("sp", "ddbg", lambda h, p=p, gch=gch: h.dma_start(
                        dbg["d_gcs"][:, p * S:(p + 1) * S], gch[:, :]), r=(gres,))

            for jc in range(1, 4):
                em.op("dve", lambda h, jc=jc: h.memset(s_tile[jc][:, 0:jc * 128], 0.0),
                      w=(f"stile{jc}",))
            emit_proj(0)
            for p in range(1, NPAIR):
                emit_retention(p - 1)
                emit_proj(p)
                emit_gn_gate(p - 1)
            emit_retention(NPAIR - 1)
            emit_gn_gate(NPAIR - 1)
            em.mark("gcs_allA", "dgc0")
            em.mark("gcs_allB", "dgc1")
        em.barrier()
        if phase_limit < 4:
            em.flush(); _final_waits(em); return nc

        with ExitStack() as ph:
            psb = lambda name, shape, dt: ph.enter_context(nc.sbuf_tensor(name, list(shape), dt))
            pps = lambda name, shape, dt: ph.enter_context(nc.psum_tensor(name, list(shape), dt))
            wo_sb = psb("wo_sb", [128, NKC * 1024], BF16)
            gcd = [psb(f"gcd{i}", [128, 1024], BF16) for i in range(4)]
            xl2 = [psb(f"xl2{i}", [128, D], F32) for i in range(4)]
            ostg = [psb(f"ostg{i}", [128, D], F32) for i in range(2)]
            pW = [pps(f"pW{i}", [128, 1024], F32) for i in range(2)]
            pgb = [pps(f"pg{i}", [128, 512], F32) for i in range(2)]
            ptg = pps("ptg", [128, 1024], BF16)
            exb = [psb(f"ex{i}", [128, 8], F32) for i in range(3)]
            mskb = [psb(f"msk{i}", [128, 8], F32) for i in range(3)]
            gstb = [psb(f"gst{i}", [128, 8], F32) for i in range(3)]
            def emit_gate_tc(tc):
                g3 = tc % 3
                ex = exb[g3]
                msk = mskb[g3]
                gst = gstb[g3]
                pg = pgb[tc % 2]
                XS = f"_{g3}"
                PGR = f"pg{tc % 2}"
                def logits(h, t=tc):
                    for kc in range(NKC):
                        h.matmul(pg[:, 0:8], xnT[:, kc * S + t * 128: kc * S + t * 128 + 128],
                                 gwT[:, kc * 8:(kc + 1) * 8], start=(kc == 0), stop=False)
                    return h.matmul(pg[:, 0:8], ones16[0:1, 0:128], gbrow[0:1, :],
                                    start=False, stop=True)
                em.op("pe", logits, r=("xnT", "gwT", "consts", "ones16"), w=(PGR,))
                em.op("dve", lambda h: h.reduce_max(gst[:, 1:2], pg[:, 0:8], axis=AX.X, negate=True),
                      r=(PGR,), w=("gst1" + XS,))
                em.op("act", lambda h: h.activation(ex[:, :], pg[:, 0:8], AF.Exp, bias=gst[:, 1:2],
                                                    accum_out=gst[:, 2:3]),
                      r=(PGR, "gst1" + XS), w=("ex" + XS, "gst2" + XS)),
                em.op("dve", lambda h: h.reciprocal(gst[:, 3:4], gst[:, 2:3]),
                      r=("gst2" + XS,), w=("gst3" + XS,))
                em.op("dve", lambda h: h.reduce_max(gst[:, 4:5], ex[:, :], axis=AX.X),
                      r=("ex" + XS,), w=("gst4" + XS,))
                em.op("dve", lambda h: h.tensor_scalar(msk[:, :], ex[:, :], gst[:, 4:5], None, ALU.is_equal),
                      r=("ex" + XS, "gst4" + XS), w=("msk" + XS,))
                em.op("dve", lambda h: h.scalar_tensor_tensor(msk[:, :], msk[:, :], -1e30, ex[:, :],
                                                              ALU.mult, ALU.add),
                      r=("msk" + XS, "ex" + XS), w=("msk" + XS,))
                em.op("dve", lambda h: h.reduce_max(gst[:, 5:6], msk[:, :], axis=AX.X),
                      r=("msk" + XS,), w=("gst5" + XS,))
                em.op("dve", lambda h: h.tensor_scalar(msk[:, :], ex[:, :], gst[:, 5:6], None, ALU.is_ge),
                      r=("ex" + XS, "gst5" + XS), w=("msk" + XS,))
                em.op("dve", lambda h: h.tensor_tensor(msk[:, :], ex[:, :], msk[:, :], ALU.mult),
                      r=("ex" + XS, "msk" + XS), w=("msk" + XS,))
                em.op("dve", lambda h, t=tc: h.tensor_scalar(g_tm[:, t * 8:(t + 1) * 8], msk[:, :],
                                                             gst[:, 3:4], None, ALU.mult),
                      r=("msk" + XS, "gst3" + XS), w=("g_tm",))
                pass



            em.dma("pool", "dwo", lambda h: h.dma_start(
                wo_sb[:, :].rearrange("p (c n) -> p c n", c=8),
                ap["wo"].rearrange("(c p) n -> p c n", p=128)), w=("wo_sb",))
            for tc in range(NTC):
                pi = tc & 1
                qi = tc % 4
                em.dma("sp", f"dx2{qi}", lambda h, t=tc, b=xl2[qi]: h.dma_start(
                    b[:, :], ap["x"][t * 128:(t + 1) * 128, :]), w=(f"xl2{qi}",))
                em.dma("sp", f"dgd{qi}", lambda h, t=tc, b=gcd[qi]: h.dma_start(
                    b[:, :], gcs[:, t * 1024:(t + 1) * 1024]),
                    r=("gcs_allA", "gcs_allB"), w=(f"gcd{qi}",))
                for half in range(2):
                    def wog(h, t=tc, half=half, b=gcd[qi], pWt=pW[pi]):
                        for kc in range(NKC):
                            h.matmul(pWt[:, half * 512:(half + 1) * 512],
                                     b[:, kc * 128:(kc + 1) * 128],
                                     wo_sb[:, kc * 1024 + half * 512: kc * 1024 + (half + 1) * 512],
                                     start=(kc == 0), stop=False)
                        return h.matmul(pWt[:, half * 512:(half + 1) * 512], ones16[0:1, 0:128],
                                        borow[0:1, half * 512:(half + 1) * 512],
                                        start=False, stop=True)
                    em.op("pe", wog, r=(f"gcd{qi}", "wo_sb", "consts", "ones16"), w=(f"pW{pi}h{half}",))
                    em.op("dve", lambda h, half=half, pWt=pW[pi], x=xl2[qi], o=ostg[pi]:
                          h.tensor_tensor(o[:, half * 512:(half + 1) * 512],
                                          pWt[:, half * 512:(half + 1) * 512],
                                          x[:, half * 512:(half + 1) * 512], ALU.add),
                          r=(f"pW{pi}h{half}", f"xl2{qi}"), w=(f"ostg{pi}",))
                em.dma("act", f"dow{pi}", lambda h, t=tc, b=ostg[pi]: h.dma_start(
                    out[t * 128:(t + 1) * 128, :], b[:, :]), r=(f"ostg{pi}",), w=(f"outp{tc}",))
                emit_gate_tc(tc)
            for tc in range(NTC):
                em.op("pe", lambda h, t=tc: h.transpose(ptg[0:8, (t % 8) * 128:(t % 8 + 1) * 128],
                                                        g_tm[:, t * 8:(t + 1) * 8], ident16[:, :]),
                      r=("g_tm", "consts"), w=("ptg",) if tc % 8 == 0 else ("ptgx",))
                if tc % 8 == 7:
                    em.op("act", lambda h, t=tc: h.copy(
                        gT8[0:8, (t - 7) * 128:(t + 1) * 128], ptg[0:8, :]),
                        r=("ptg", "ptgx"), w=("gT8",))
            em.mark("outp_allA", "dow0")
            em.mark("outp_allB", "dow1")
        em.barrier()
        if phase_limit < 5:
            em.flush(); _final_waits(em); return nc

        with ExitStack() as ph:
            psb = lambda name, shape, dt: ph.enter_context(nc.sbuf_tensor(name, list(shape), dt))
            pps = lambda name, shape, dt: ph.enter_context(nc.psum_tensor(name, list(shape), dt))
            yacc = psb("yacc", [128, 8 * 1024], F32)
            gh = [psb(f"gh{i}", [128, 4 * 1024], BF16) for i in range(2)]
            w1s = [psb(f"w1s{i}", [128, 4096], BF16) for i in range(4)]
            w2s = [psb(f"w2s{i}", [128, 4096], BF16) for i in range(2)]
            gbc = [psb(f"gbc{i}", [128, 1024], BF16) for i in range(8)]
            ghr = [psb(f"ghr{i}", [128, 512], BF16) for i in range(2)]
            pH = [pps(f"pH{i}", [128, 512], F32) for i in range(2)]
            pY = [pps(f"pY{i}", [128, 1024], F32) for i in range(2)]
            pBC = pps("pBC", [128, 512], F32)

            for pp in range(NPASS):
                T0 = pp * PASS
                em.dma("sp", "dyi", lambda h, T0=T0: h.dma_start(
                    yacc[:, :].rearrange("p (c o) -> p c o", c=8), ap_out_slice(out, T0)),
                    r=("outp_allA", "outp_allB") + tuple(f"yacc{i}" for i in range(8)),
                    w=tuple(f"yacc{i}" for i in range(8)))
                for e in range(8):
                    for half in range(2):
                        em.op("pe", lambda h, e=e, half=half, T0=T0: h.matmul(
                            pBC[:, :], sel8[0:8, e * 128:(e + 1) * 128],
                            gT8[0:8, T0 + half * 512: T0 + (half + 1) * 512],
                            start=True, stop=True),
                            r=("gT8", "consts"), w=("pBC",))
                        em.op("act", lambda h, e=e, half=half: h.copy(
                            gbc[e][:, half * 512:(half + 1) * 512], pBC[:, :]),
                            r=("pBC",), w=(f"gbc{e}",))
                for hc in range(NHC):
                    for eg in range(4):
                        for e4 in range(2):
                            e = eg * 2 + e4
                            wi = (hc * 8 + e) % 4
                            em.dma("pool", f"dw1{wi}", lambda h, e=e, hc=hc, b=w1s[wi]: h.dma_start(
                                b[:, :].rearrange("p (c m) -> p c m", c=8),
                                ap["w1"][e][:, hc * 512:(hc + 1) * 512].rearrange(
                                    "(c p) m -> p c m", p=128)), w=(f"w1s{wi}",))
                            em.dma("pool", f"dw2{e4}", lambda h, e=e, hc=hc, b=w2s[e4]: h.dma_start(
                                b[:, :].rearrange("p (c o) -> p c o", c=4),
                                ap["w2"][e][hc * 512:(hc + 1) * 512, :].rearrange(
                                    "(c p) o -> p c o", p=128)), w=(f"w2s{e4}",))
                            for m in range(4):
                                for nt in range(2):
                                    hi = (m * 2 + nt) & 1
                                    def mm1(h, m=m, nt=nt, T0=T0, b=w1s[wi], pHt=pH[hi]):
                                        last = None
                                        for kc in range(NKC):
                                            last = h.matmul(
                                                pHt[:, :],
                                                b[:, kc * 512 + m * 128: kc * 512 + (m + 1) * 128],
                                                xnT[:, kc * S + T0 + nt * 512: kc * S + T0 + (nt + 1) * 512],
                                                start=(kc == 0), stop=(kc == NKC - 1))
                                        return last
                                    em.op("pe", mm1, r=(f"w1s{wi}", "xnT"), w=(f"pH{hi}",))
                                    bidx = (hc * 4 + m) * 8 + e
                                    em.op("act", lambda h, pHt=pH[hi], o=ghr[hi], bi=bidx: h.activation(
                                        o[:, :], pHt[:, :], AF.Relu, bias=b1T[:, bi:bi + 1]),
                                        r=(f"pH{hi}", "consts"), w=(f"ghr{hi}",))
                                    em.op("dve", lambda h, o=gh[e4], m=m, nt=nt, i=ghr[hi], e=e:
                                          h.tensor_tensor(
                                              o[:, m * 1024 + nt * 512: m * 1024 + (nt + 1) * 512],
                                              i[:, :], gbc[e][:, nt * 512:(nt + 1) * 512], ALU.mult),
                                          r=(f"ghr{hi}", f"gbc{e}"), w=(f"gh{e4}",))
                        for tc in range(8):
                            yi = tc & 1
                            for half in range(2):
                                def mm2a(h, tc=tc, half=half, hc=hc, eg=eg, T0=T0, pYt=pY[yi]):
                                    first = True
                                    last = None
                                    if hc == 0 and eg == 0:
                                        last = h.matmul(pYt[:, half * 512:(half + 1) * 512],
                                                        gT8[0:8, T0 + tc * 128: T0 + (tc + 1) * 128],
                                                        b2sb[0:8, half * 512:(half + 1) * 512],
                                                        start=True, stop=False)
                                        first = False
                                    for m in range(4):
                                        last = h.matmul(
                                            pYt[:, half * 512:(half + 1) * 512],
                                            gh[0][:, m * 1024 + tc * 128: m * 1024 + tc * 128 + 128],
                                            w2s[0][:, m * 1024 + half * 512: m * 1024 + (half + 1) * 512],
                                            start=first, stop=False)
                                        first = False
                                    return last
                                def mm2b(h, tc=tc, half=half, pYt=pY[yi]):
                                    last = None
                                    for m in range(4):
                                        last = h.matmul(
                                            pYt[:, half * 512:(half + 1) * 512],
                                            gh[1][:, m * 1024 + tc * 128: m * 1024 + tc * 128 + 128],
                                            w2s[1][:, m * 1024 + half * 512: m * 1024 + (half + 1) * 512],
                                            start=False, stop=(m == 3))
                                    return last
                                em.op("pe", mm2a,
                                      r=("gh0", "w2s0", "gT8", "consts"), w=(f"pY{yi}h{half}",))
                                em.op("pe", mm2b,
                                      r=("gh1", "w2s1"), w=(f"pY{yi}h{half}",))
                            em.op("dve", lambda h, tc=tc, pYt=pY[yi]: h.tensor_tensor(
                                yacc[:, tc * 1024:(tc + 1) * 1024], pYt[:, :],
                                yacc[:, tc * 1024:(tc + 1) * 1024], ALU.add),
                                r=(f"pY{yi}h0", f"pY{yi}h1", f"yacc{tc}"), w=(f"yacc{tc}",))
                            if hc == NHC - 1 and eg == 3:
                                em.dma("sp", "dpw", lambda h, T0=T0, tc=tc: h.dma_start(
                                    out[T0 + tc * 128:T0 + (tc + 1) * 128, :],
                                    yacc[:, tc * 1024:(tc + 1) * 1024]),
                                    r=(f"yacc{tc}",), w=("yacc_st",))
        em.flush()
        _final_waits(em)
    return nc


def ap_out_slice(out, T0):
    return out[T0:T0 + PASS, :].rearrange("(c p) o -> p c o", p=128)


def _final_waits(em):
    with em.nc.Block() as blk:
        sems = [(em.sem[n], em.cnt[n]) for n in em.sem
                if n.startswith(("dow", "dpw", "ddbg", "dgc")) and em.cnt[n] > 0]
        def _(sy):
            for s, v in sems:
                sy.wait_ge(s, v)
        blk.sync(_)


def _prep(inputs):
    c = make_consts()
    W = inputs
    base = dict(
        wq=W["L0_attn_Wq"], wk=W["L0_attn_Wk"], wv=W["L0_attn_Wv"],
        wg=W["L0_attn_Wg"], wo=W["L0_attn_Wo"],
        w1=W["L0_ffn_w1"], w2=W["L0_ffn_w2"],
        bvrow=np.ascontiguousarray(np.asarray(W["L0_attn_bv"]).reshape(1, D)).astype(BF),
        borow=np.ascontiguousarray(np.asarray(W["L0_attn_bo"]).reshape(1, D)).astype(BF),
        gbrow=np.ascontiguousarray(np.asarray(W["L0_ffn_gate_b"]).reshape(1, 8)).astype(BF),
        bT=np.ascontiguousarray(
            np.stack([W["L0_attn_bq"], W["L0_attn_bk"], W["L0_attn_bg"]])
            .reshape(3, 8, 128).transpose(2, 1, 0).reshape(128, 24)).astype(np.float32),
        b1T=np.ascontiguousarray(
            np.asarray(W["L0_ffn_b1"]).reshape(8, 32, 128).transpose(2, 1, 0)
            .reshape(128, 256)).astype(np.float32),
        b2sb=np.ascontiguousarray(W["L0_ffn_b2"]).astype(BF),
        gnw2=np.ascontiguousarray(np.asarray(W["L0_attn_gn_w"]).reshape(8, 2).T).astype(np.float32),
        gnb2=np.ascontiguousarray(np.asarray(W["L0_attn_gn_b"]).reshape(8, 2).T).astype(np.float32),
        gate_wT=np.ascontiguousarray(np.asarray(W["L0_ffn_gate_w"]).T).astype(np.float32),
        tqa=c["tqa"], tqb=c["tqb"], tka=c["tka"], tkb=c["tkb"],
        maskbig=c["maskbig"], ident16=c["ident16"], shf=c["shf"],
        e2=c["e2"], p2=c["p2"], sel8=c["sel8"], gamC=c["gamC"],
    )
    for k in ("wq", "wk", "wv", "wg", "wo", "w1", "w2"):
        base[k] = np.ascontiguousarray(np.asarray(base[k], np.float32))
    return base


def kernel(**inputs):
    base = _prep(inputs)
    nc = build()
    x = np.asarray(inputs["L0_attn_x"], np.float32)
    in_maps = [dict(base, x=np.ascontiguousarray(x[cid])) for cid in range(B)]
    res = run_bass_kernel_spmd(nc, in_maps, core_ids=list(range(B)))
    return np.stack([np.asarray(res.results[cid]["out"]) for cid in range(B)])
